# version 10
# baseline (speedup 1.0000x reference)
import numpy as np
import ml_dtypes
from contextlib import ExitStack

import concourse.bass as bass
import concourse.mybir as mybir
from concourse.bass_utils import run_bass_kernel_spmd

F32 = mybir.dt.float32
BF16 = mybir.dt.bfloat16
F16 = mybir.dt.float16
I32 = mybir.dt.int32
AF = mybir.ActivationFunctionType
ALU = mybir.AluOpType
AXL = mybir.AxisListType
BF = ml_dtypes.bfloat16

C = 1024
NH = 8
BLK = 128
HF = 4096
LOGM = -1.0e5
LN128_HALF = 0.5 * float(np.log(128.0))


class Tile:
    __slots__ = ("h", "writer", "readers", "name")

    def __init__(self, h, name):
        self.h = h
        self.name = name
        self.writer = None
        self.readers = {}

    def __getitem__(self, idx):
        return self.h[idx]

    @property
    def ap(self):
        return self.h[:]


class Ker:
    ENG = {"sp": "sync", "pe": "tensor", "act": "scalar", "dve": "vector", "pool": "gpsimd"}

    def __init__(self, nc, ctx):
        self.nc = nc
        self.ctx = ctx
        self.sems = {}
        self.counts = {}
        self.watermark = {}
        self.tape = {e: [] for e in self.ENG}
        for e in self.ENG:
            self._mksem("s_" + e)
        self.streams = {}

    def _mksem(self, name):
        self.sems[name] = self.ctx.enter_context(self.nc.semaphore(name))
        self.counts[name] = 0

    def stream(self, name, nslots=3):
        slots = []
        for i in range(nslots):
            sn = f"d_{name}{i}"
            self._mksem(sn)
            slots.append(sn)
        self.streams[name] = [slots, 0]

    def sb(self, name, shape, dtype):
        return Tile(self.ctx.enter_context(self.nc.sbuf_tensor(name, list(shape), dtype)), name)

    def ps(self, name, shape, dtype=F32):
        return Tile(self.ctx.enter_context(self.nc.psum_tensor(name, list(shape), dtype)), name)

    def _dep_waits(self, ename, reads, writes):
        waits = []

        def add(tok):
            if tok is None:
                return
            sname, val = tok
            if val <= 0:
                return
            key = (ename, sname)
            if self.watermark.get(key, 0) >= val:
                return
            self.watermark[key] = val
            waits.append((sname, val))
        for t in reads:
            add(t.writer)
        for t in writes:
            add(t.writer)
            for sn, val in t.readers.items():
                add((sn, val))
        return waits

    def begin_defer(self):
        self._defer = []

    def end_defer(self):
        d = self._defer
        self._defer = None
        return d

    def run(self, ename, call, reads=(), writes=(), inc=True):
        if getattr(self, "_defer", None) is not None:
            self._defer.append(lambda ename=ename, call=call, reads=tuple(reads), writes=tuple(writes), inc=inc:
                               self._run_now(ename, call, reads, writes, inc))
            return None
        return self._run_now(ename, call, reads, writes, inc)

    def _run_now(self, ename, call, reads=(), writes=(), inc=True):
        waits = self._dep_waits(ename, reads, writes)
        sname = "s_" + ename
        if inc:
            self.counts[sname] += 1
            incv = (self.sems[sname], 1)
        else:
            incv = None
        self.tape[ename].append((waits, call, incv))
        tok = (sname, self.counts[sname])
        for t in writes:
            t.writer = tok
            t.readers = {}
        for t in reads:
            if t.readers.get(tok[0], 0) < tok[1]:
                t.readers[tok[0]] = tok[1]
        return tok

    def dma(self, stream, out_ap, in_ap, reads=(), writes=(), ename="sp", transpose=False):
        if getattr(self, "_defer", None) is not None:
            self._defer.append(lambda stream=stream, out_ap=out_ap, in_ap=in_ap, reads=tuple(reads),
                               writes=tuple(writes), ename=ename, transpose=transpose:
                               self._dma_now(stream, out_ap, in_ap, reads, writes, ename, transpose))
            return None
        return self._dma_now(stream, out_ap, in_ap, reads, writes, ename, transpose)

    def _dma_now(self, stream, out_ap, in_ap, reads=(), writes=(), ename="sp", transpose=False):
        waits = self._dep_waits(ename, reads, writes)
        slots, idx = self.streams[stream]
        sname = slots[idx % len(slots)]
        self.streams[stream][1] = idx + 1
        prior = self.counts[sname]
        if prior > 0 and self.watermark.get((ename, sname), 0) < prior:
            self.watermark[(ename, sname)] = prior
            waits.append((sname, prior))
        self.counts[sname] += 16

        def call(eng, o=out_ap, i=in_ap, tr=transpose):
            if tr:
                return eng.dma_start_transpose(out=o, in_=i)
            return eng.dma_start(out=o, in_=i)
        self.tape[ename].append((waits, call, (self.sems[sname], 16)))
        tok = (sname, self.counts[sname])
        for t in writes:
            t.writer = tok
            t.readers = {}
        for t in reads:
            if t.readers.get(tok[0], 0) < tok[1]:
                t.readers[tok[0]] = tok[1]
        return tok

    def barrier(self):
        snap = dict(self.counts)
        for e in self.ENG:
            waits = []
            for sname, val in snap.items():
                if val <= 0:
                    continue
                key = (e, sname)
                if self.watermark.get(key, 0) >= val:
                    continue
                self.watermark[key] = val
                waits.append((sname, val))
            if waits:
                self.tape[e].append((waits, None, None))

    def finalize(self, block):
        for ename, attr in self.ENG.items():
            entries = self.tape[ename]
            deco = getattr(block, attr)

            def body(eng, entries=entries):
                for waits, call, incv in entries:
                    for sname, val in waits:
                        eng.wait_ge(self.sems[sname], val)
                    if call is not None:
                        inst = call(eng)
                        if incv is not None:
                            inst.then_inc(*incv)
            deco(body)


def host_prep(inputs, nb_main, nb_pref):
    x = np.ascontiguousarray(np.asarray(inputs["L0_attn_x"], dtype=np.float32))
    Wqkvz = np.asarray(inputs["L0_attn_W_qkvz"], dtype=np.float32)
    Wba = np.asarray(inputs["L0_attn_W_ba"], dtype=np.float32)
    convw = np.asarray(inputs["L0_attn_conv_w"], dtype=np.float32)[:, 0, :]
    dtb = np.asarray(inputs["L0_attn_dt_bias"], dtype=np.float32)
    alog = np.asarray(inputs["L0_attn_A_log"], dtype=np.float32)
    nw = np.asarray(inputs["L0_attn_norm_weight"], dtype=np.float32)
    Wout = np.asarray(inputs["L0_attn_W_out"], dtype=np.float32)
    W1 = np.asarray(inputs["L0_ffn_W1"], dtype=np.float32)
    b1 = np.asarray(inputs["L0_ffn_b1"], dtype=np.float32)
    W2 = np.asarray(inputs["L0_ffn_W2"], dtype=np.float32)
    b2 = np.asarray(inputs["L0_ffn_b2"], dtype=np.float32)

    cols = [Wqkvz[:, 512 * h + off:512 * h + off + 128] for off in (0, 128, 256) for h in range(NH)]
    qkv = np.concatenate(cols, axis=1)
    ba = np.concatenate([Wba[:, 0::2], Wba[:, 1::2]], axis=1)
    wqkv = np.concatenate([qkv, ba], axis=1)
    wqkv = np.ascontiguousarray(wqkv.reshape(8, 128, 3088).transpose(1, 0, 2).reshape(128, 8 * 3088))
    wz = np.concatenate([Wqkvz[:, 512 * h + 384:512 * h + 512] for h in range(NH)], axis=1)
    wz = np.ascontiguousarray(wz.reshape(8, 128, 1024).transpose(1, 0, 2).reshape(128, 8 * 1024))
    convw_r = np.ascontiguousarray(convw.reshape(24, 128, 4).transpose(1, 0, 2).reshape(128, 96))
    wout_r = np.ascontiguousarray(Wout.reshape(8, 128, 1024).transpose(1, 0, 2).reshape(128, 8 * 1024))
    w1t = np.ascontiguousarray(W1.T.reshape(8, 128, 4096).transpose(1, 0, 2).reshape(128, 8 * 4096))
    w2t = np.ascontiguousarray(W2.T.reshape(32, 128, 1024).transpose(1, 0, 2).reshape(128, 32 * 1024))

    ar = np.arange(128)
    lmS = np.where(ar[None, :] > ar[:, None], 0.0, LOGM).astype(np.float32)
    lmI = np.where(ar[None, :] >= ar[:, None], -LN128_HALF, LOGM).astype(np.float32)
    sel = np.zeros((8, 1024), np.float32)
    for r in range(8):
        sel[r, 128 * r:128 * (r + 1)] = 1.0

    shared = dict(
        wqkv=wqkv, wz=wz, convw=convw_r, wout=wout_r,
        w1t=w1t, b1=b1.reshape(1, HF), w2t=w2t, b2=b2.reshape(1, C),
        dtb=dtb.reshape(8, 1), alog=alog.reshape(8, 1), nwc=nw.reshape(128, 1),
        idf=np.eye(128, dtype=np.float32), idb=np.eye(128).astype(BF),
        lmS=lmS.astype(BF), lmI=lmI.astype(BF), sel=sel,
    )
    Tm, Tp = nb_main * BLK, nb_pref * BLK
    per_core = []
    for b in range(x.shape[0]):
        for half in range(2):
            t0 = half * Tm
            xm = np.zeros((Tm + 3, C), np.float32)
            xm[3:] = x[b, t0:t0 + Tm]
            if half == 1:
                xm[:3] = x[b, t0 - 3:t0]
            xp = np.zeros((Tp + 3, C), np.float32)
            if half == 1:
                xp[3:] = x[b, 0:Tp]
            d = dict(shared)
            d["xm"] = xm
            d["xp"] = xp
            per_core.append(d)
    return per_core


def build_program(nb_main, nb_pref):
    nc = bass.Bass("TRN2", target_bir_lowering=False)
    Tm, Tp = nb_main * BLK, nb_pref * BLK

    def din(name, shape, dt=F32):
        return nc.dram_tensor(name, list(shape), dt, kind="ExternalInput").ap()

    xm_d = din("xm", [Tm + 3, C])
    xp_d = din("xp", [Tp + 3, C])
    wqkv_d = din("wqkv", [128, 8 * 3088])
    wz_d = din("wz", [128, 8 * 1024])
    convw_d = din("convw", [128, 96])
    wout_d = din("wout", [128, 8 * 1024])
    w1t_d = din("w1t", [128, 8 * 4096])
    b1_d = din("b1", [1, HF])
    w2t_d = din("w2t", [128, 32 * 1024])
    b2_d = din("b2", [1, C])
    dtb_d = din("dtb", [8, 1])
    alog_d = din("alog", [8, 1])
    nwc_d = din("nwc", [128, 1])
    idf_d = din("idf", [128, 128])
    idb_d = din("idb", [128, 128], BF16)
    lmS_d = din("lmS", [128, 128], BF16)
    lmI_d = din("lmI", [128, 128], BF16)
    sel_d = din("sel", [8, 1024])
    out_d = nc.dram_tensor("out", [Tm, C], F32, kind="ExternalOutput").ap()
    xmid_d = nc.dram_tensor("xmid", [Tm, C], F32).ap()
    w1q_d = nc.dram_tensor("w1q", [128, 8 * 4096], F16).ap()
    w2q_d = nc.dram_tensor("w2q", [128, 32 * 1024], F16).ap()

    ctx = ExitStack()
    with ctx:
        k = Ker(nc, ctx)
        xmid_t = Tile(None, "xmid_t")
        w1q_t = Tile(None, "w1q_t")
        w2q_t = Tile(None, "w2q_t")
        for s, n in [("wl", 2), ("xl", 3), ("tq", 4), ("sp1", 2), ("st", 2), ("cw", 5)]:
            k.stream(s, n)

        idf = k.sb("idf_s", [128, 128], F32)
        dtb = k.sb("dtb_s", [8, 1], F32)
        alog = k.sb("alog_s", [8, 1], F32)
        nea = k.sb("nea", [8, 1], F32)
        nwc = k.sb("nwc_s", [128, 1], F32)
        ones8 = k.sb("ones8", [8, 128], F32)
        onesc = k.sb("onesc", [128, 1], F32)
        onesr = k.sb("onesr", [1, 128], F32)
        cc = k.sb("cc", [128, 8], F32)
        c8 = k.sb("c8", [8, 4], F32)
        w1s = k.sb("w1s", [128, 2], F32)
        w2s = k.sb("w2s", [128, 2], F32)
        bbrow = k.sb("bbrow", [1, HF + C], BF16)
        stat = k.sb("stat", [128, 8], F32)
        junk = k.sb("junk", [128, 128], BF16)
        junk2 = k.sb("junk2", [1, 4], F32)
        pj = [k.ps(f"pj{i}", [128, 512]) for i in range(2)]
        pcv0 = k.ps("pcv0", [128, 512])
        pcv = [pcv0, pcv0]
        pc1 = k.ps("pc1", [128, 512])
        pc2 = k.ps("pc2", [128, 512])
        pst = k.ps("pst", [128, 512])
        pstb = k.ps("pstb", [128, 512], BF16)

        block = ctx.enter_context(nc.Block(no_gpsimd_drain=True))

        k.dma("wl", idf.ap, idf_d, writes=[idf])
        k.dma("wl", dtb.ap, dtb_d, writes=[dtb])
        k.dma("wl", alog.ap, alog_d, writes=[alog])
        k.dma("wl", nwc.ap, nwc_d, writes=[nwc])
        k.run("dve", lambda e: e.memset(ones8.ap, 1.0), writes=[ones8])
        k.run("dve", lambda e: e.memset(onesc.ap, 1.0), writes=[onesc])
        k.run("dve", lambda e: e.memset(onesr.ap, 1.0), writes=[onesr])
        k.dma("cw", bbrow[0:1, 0:HF], b1_d, writes=[bbrow], ename="pool")
        k.dma("cw", bbrow[0:1, HF:HF + C], b2_d, writes=[bbrow], ename="pool")
        for col, val in [(0, 1e-5), (1, 1e-6), (2, 1.0), (3, 256.5), (4, 0.0), (5, 1.0 / C), (6, 1.0 / 128.0), (7, -1.0)]:
            k.run("dve", lambda e, col=col, val=val: e.memset(cc[:, col:col + 1], val), writes=[cc])
        for col, val in [(0, 1.0), (1, -LN128_HALF), (2, -1.0), (3, 0.0)]:
            k.run("dve", lambda e, col=col, val=val: e.memset(c8[:, col:col + 1], val), writes=[c8])
        k.run("act", lambda e: e.activation(out=nea.ap, in_=alog.ap, func=AF.Exp), reads=[alog], writes=[nea])
        k.run("dve", lambda e: e.tensor_scalar_mul(out=nea.ap, in0=nea.ap, scalar1=-1.0), reads=[nea], writes=[nea])

        with ExitStack() as tctx:
            CH = 8192
            stg = Tile(tctx.enter_context(nc.sbuf_tensor("stg", [128, CH], F32)), "stg")
            gp_ = Tile(tctx.enter_context(nc.sbuf_tensor("gp_", [128, CH], BF16)), "gp_")
            gn_ = Tile(tctx.enter_context(nc.sbuf_tensor("gn_", [128, CH], BF16)), "gn_")
            qw = Tile(tctx.enter_context(nc.sbuf_tensor("qw", [128, CH], F16)), "qw")
            for wt_d, wq_d, wsT in ((w1t_d, w1q_d, w1s), (w2t_d, w2q_d, w2s)):
                nelem = 8 * 4096
                nch = nelem // CH
                for ci in range(nch):
                    k.dma("wl", stg.ap, wt_d[:, CH * ci:CH * (ci + 1)], writes=[stg])
                    k.run("act", lambda e, s=stg, ci=ci: e.activation(
                        out=s.ap, in_=s.ap, func=AF.Abs, accum_out=stat[:, ci:ci + 1]),
                        reads=[stg], writes=[stg, stat])
                k.run("pe", lambda e, nch=nch: e.matmul(pst[0:1, 0:1], onesc.ap, stat[:, 0:nch],
                                                        start=True, stop=True),
                      reads=[onesc, stat], writes=[pst]) if False else None
                k.run("dve", lambda e, nch=nch: e.tensor_reduce(out=stat[:, 4:5], in_=stat[:, 0:nch],
                                                                axis=AXL.X, op=ALU.add),
                      reads=[stat], writes=[stat])
                k.run("pe", lambda e: e.matmul(pst[0:1, 0:1], onesc.ap, stat[:, 4:5], start=True, stop=True),
                      reads=[onesc, stat], writes=[pst])
                k.run("dve", lambda e, n=nelem: e.tensor_scalar(out=junk2[0:1, 0:1], in0=pst[0:1, 0:1],
                                                                scalar1=1.0 / (128.0 * n), scalar2=1e-5,
                                                                op0=ALU.mult, op1=ALU.max),
                      reads=[pst], writes=[junk2])
                k.run("pe", lambda e: e.matmul(pst[0:128, 1:2], onesr.ap, junk2[0:1, 0:1], start=True, stop=True),
                      reads=[onesr, junk2], writes=[pst])
                k.run("dve", lambda e: e.tensor_copy(out=stat[:, 1:2], in_=pst[0:128, 1:2]),
                      reads=[pst], writes=[stat])
                k.run("dve", lambda e, w=wsT: e.tensor_copy(out=w[:, 0:1], in_=stat[:, 1:2]), reads=[stat], writes=[wsT])
                k.run("dve", lambda e, w=wsT: e.reciprocal(out=w[:, 1:2], in_=stat[:, 1:2]), reads=[stat], writes=[wsT])
                k.run("dve", lambda e: e.tensor_scalar_mul(out=stat[:, 2:3], in0=stat[:, 1:2], scalar1=0.5),
                      reads=[stat], writes=[stat])
                k.run("dve", lambda e: e.tensor_scalar_mul(out=stat[:, 3:4], in0=stat[:, 1:2], scalar1=-0.5),
                      reads=[stat], writes=[stat])
                for ci in range(nch):
                    k.dma("wl", stg.ap, wt_d[:, CH * ci:CH * (ci + 1)], writes=[stg])
                    k.run("dve", lambda e, s=stg: e.tensor_tensor(out=gp_.ap, in0=s.ap,
                                                                  in1=stat[:, 2:3].broadcast_to((128, CH)),
                                                                  op=ALU.is_ge),
                          reads=[stg, stat], writes=[gp_])
                    k.run("dve", lambda e, s=stg: e.tensor_tensor(out=gn_.ap, in0=s.ap,
                                                                  in1=stat[:, 3:4].broadcast_to((128, CH)),
                                                                  op=ALU.is_le),
                          reads=[stg, stat], writes=[gn_])
                    k.run("dve", lambda e: e.tensor_tensor(out=qw.ap, in0=gp_.ap, in1=gn_.ap, op=ALU.subtract),
                          reads=[gp_, gn_], writes=[qw])
                    k.dma("sp1", wq_d[:, CH * ci:CH * (ci + 1)], qw.ap, reads=[qw], writes=[w1q_t if wq_d is w1q_d else w2q_t])
            k.barrier()

        with ExitStack() as actx:
            def asb(name, shape, dt):
                return Tile(actx.enter_context(nc.sbuf_tensor(name, list(shape), dt)), name)

            idb = asb("idb_s", [128, 128], BF16)
            lmS = asb("lmS_s", [128, 128], BF16)
            lmI = asb("lmI_s", [128, 128], BF16)
            sel = asb("sel_s", [8, 1024], F32)
            xtile = [asb(f"xt{i}", [128, C], F32) for i in range(2)]
            k.dma("wl", idb.ap, idb_d, writes=[idb])
            k.dma("wl", lmS.ap, lmS_d, writes=[lmS])
            k.dma("wl", lmI.ap, lmI_d, writes=[lmI])
            k.dma("wl", sel.ap, sel_d, writes=[sel])
            wqkv = asb("wqkv_s", [128, 8 * 3088], BF16)
            wz = asb("wz_s", [128, 8 * 1024], BF16)
            wout = asb("wout_s", [128, 8 * 1024], BF16)
            convw = asb("convw_s", [128, 96], F32)
            convd = asb("convd", [128, 96 * 128], BF16)
            xln = asb("xln", [128, C], BF16)
            xlnT = asb("xlnT", [128, 8 * 128], BF16)
            xctxT = asb("xctxT", [128, 8 * 4], BF16)
            convin = [asb(f"cvi{i}", [128, 24 * 131], BF16) for i in range(2)]
            qkvF_r = [asb(f"qkvF{i}", [128, 24 * 128], BF16) for i in range(2)]
            qkvT_one = asb("qkvT", [128, 24 * 128], BF16)
            qkvT_r = [qkvT_one, qkvT_one]
            _shared_stats = dict(
                bF=asb("bFs", [8, 128], F32), aF=asb("aFs", [8, 128], F32),
                ssq=asb("ssqs", [128, 16], F32), lss=asb("lsss", [128, 16], F32),
                lskF=asb("lskFs", [8, 128], F32), lsqF=asb("lsqFs", [8, 128], F32),
                gF=asb("gFs", [8, 128], F32), gcF=asb("gcFs", [8, 128], F32),
                spbF=asb("spbFs", [8, 128], F32), ngcF=asb("ngcFs", [8, 128], F32),
                tmp8=asb("tmp8s", [8, 128], F32), expt=asb("expts", [8, 128], F32),
                stat=asb("stats_", [128, 8], F32),
            )
            sring = []
            for i_ in range(2):
                d_ = dict(_shared_stats)
                d_.update(
                    E1t=asb(f"E1t{i_}", [8, 128], F32), E2t=asb(f"E2t{i_}", [8, 128], F32),
                    E3t=asb(f"E3t{i_}", [8, 128], F32),
                    tmcols=asb(f"tmcols{i_}", [128, 40], F32),
                )
                sring.append(d_)
            vbeta_r = [asb(f"vbeta{i}", [128, 8 * 128], BF16) for i in range(2)]
            kprime_r = [asb(f"kprime{i}", [128, 8 * 128], BF16) for i in range(2)]
            tset = []
            for par in range(4):
                tset.append(dict(
                    M1=asb(f"M1_{par}", [128, 128], BF16),
                    M2=asb(f"M2_{par}", [128, 128], BF16),
                    B0=asb(f"B0_{par}", [128, 128], BF16),
                    C0=asb(f"C0_{par}", [128, 128], BF16),
                    attnT=asb(f"attnT_{par}", [128, 128], BF16),
                    Bk=[asb(f"Bk{i}_{par}", [128, 128], BF16) for i in range(6)],
                    Ck=[asb(f"Ck{i}_{par}", [128, 128], BF16) for i in range(2)],
                    Wch=[asb(f"Wch{i}_{par}", [128, 128], BF16) for i in range(2)],
                    vnref=[None],
                ))
            Sh = [asb(f"S{h}", [128, 128], BF16) for h in range(8)]
            outb_r = [asb(f"outb{i}", [128, 8 * 128], BF16) for i in range(2)]
            zsil_r = [asb(f"zsil{i}", [128, 8 * 128], BF16) for i in range(2)]
            rms = asb("rms", [128, 8], F32)

            k.dma("cw", wqkv.ap, wqkv_d, writes=[wqkv], ename="pool")
            k.dma("cw", wz.ap, wz_d, writes=[wz], ename="pool")
            k.dma("wl", convw.ap, convw_d, writes=[convw])
            k.dma("cw", wout.ap, wout_d, writes=[wout], ename="pool")
            k.run("dve", lambda e: e.tensor_scalar(out=wout.ap, in0=wout.ap, scalar1=nwc[:, 0:1], scalar2=1.0,
                                                   op0=ALU.mult, op1=ALU.mult),
                  reads=[wout, nwc], writes=[wout])
            k.run("dve", lambda e: e.tensor_copy(
                out=convd.h[:].rearrange("p (a b) -> p a b", b=128),
                in_=convw.h[:].rearrange("p (a b) -> p a b", b=1).broadcast_to((128, 96, 128))),
                reads=[convw], writes=[convd])
            k.run("dve", lambda e: e.tensor_tensor(
                out=convd.h[:].rearrange("p (a b) -> p a b", b=128),
                in0=convd.h[:].rearrange("p (a b) -> p a b", b=128),
                in1=idb.h[:].rearrange("p (a b) -> p a b", a=1).broadcast_to((128, 96, 128)),
                op=ALU.mult),
                reads=[convd, idb], writes=[convd])
            for h in range(8):
                k.run("dve", lambda e, h=h: e.memset(Sh[h].ap, 0.0), writes=[Sh[h]])
            for i_ in range(2):
                k.run("dve", lambda e, i_=i_: e.memset(sring[i_]["E3t"].ap, 0.0), writes=[sring[i_]["E3t"]])
                k.run("dve", lambda e, i_=i_: e.memset(sring[i_]["ssq"].ap, 0.0), writes=[sring[i_]["ssq"]])
            k.run("dve", lambda e: e.memset(outb_r[0].ap, 0.0), writes=[outb_r[0]])
            k.run("dve", lambda e: e.memset(outb_r[1].ap, 0.0), writes=[outb_r[1]])
            k.run("dve", lambda e: e.memset(qkvT_one.ap, 0.0), writes=[qkvT_one])
            k.run("dve", lambda e: e.memset(convin[0].ap, 0.0), writes=[convin[0]])
            k.run("dve", lambda e: e.memset(convin[1].ap, 0.0), writes=[convin[1]])
            xctx = xtile[1]

            def prep_block(x_d, b, main):
                xt = xtile[b % 2]
                cvi = convin[b % 2]
                cvprev = convin[(b + 1) % 2]
                qkvF, qkvT = qkvF_r[b % 2], qkvT_r[b % 2]
                vbeta, kprime, zsil = vbeta_r[b % 2], kprime_r[b % 2], zsil_r[b % 2]
                sr = sring[b % 2]
                bF, aF, ssq, lss = sr["bF"], sr["aF"], sr["ssq"], sr["lss"]
                lskF, lsqF, gF, gcF = sr["lskF"], sr["lsqF"], sr["gF"], sr["gcF"]
                spbF, ngcF, tmp8 = sr["spbF"], sr["ngcF"], sr["tmp8"]
                E1t, E2t, E3t, tmcols, expt, stat = sr["E1t"], sr["E2t"], sr["E3t"], sr["tmcols"], sr["expt"], sr["stat"]
                k.dma("xl", xt.ap, x_d[3 + BLK * b: 3 + BLK * (b + 1), :], writes=[xt])
                k.run("dve", lambda e: e.tensor_reduce(out=stat[:, 0:1], in_=xt.ap, axis=AXL.X, op=ALU.add),
                      reads=[xt], writes=[stat])
                k.run("dve", lambda e: e.tensor_scalar_mul(out=stat[:, 1:2], in0=stat[:, 0:1], scalar1=-1.0 / C),
                      reads=[stat], writes=[stat])
                k.run("act", lambda e: e.activation(out=xln.ap, in_=xt.ap, func=AF.Square,
                                                    bias=stat[:, 1:2], scale=cc[:, 2:3], accum_out=stat[:, 2:3]),
                      reads=[xt, stat, cc], writes=[xln, stat])
                k.run("act", lambda e: e.activation(out=stat[:, 3:4], in_=stat[:, 2:3], func=AF.Sqrt,
                                                    bias=cc[:, 0:1], scale=cc[:, 5:6]),
                      reads=[stat, cc], writes=[stat])
                k.run("dve", lambda e: e.reciprocal(out=stat[:, 4:5], in_=stat[:, 3:4]), reads=[stat], writes=[stat])
                k.run("dve", lambda e: e.tensor_tensor(out=stat[:, 5:6], in0=stat[:, 1:2], in1=stat[:, 4:5],
                                                       op=ALU.mult), reads=[stat], writes=[stat])
                k.run("act", lambda e: e.activation(out=xln.ap, in_=xt.ap, func=AF.Identity,
                                                    bias=stat[:, 5:6], scale=stat[:, 4:5]),
                      reads=[xt, stat], writes=[xln])
                k.dma("tq", xlnT.h[:].rearrange("p (a b) -> p a b", b=128),
                      xln.h[:].rearrange("p (a b) -> p a b", b=128),
                      reads=[xln], writes=[xlnT], transpose=True)
                rows = list(range(24)) if main else list(range(8, 24))
                for ri, r in enumerate(rows):
                    p = pj[ri % 2]
                    for d in range(8):
                        k.run("pe", lambda e, r=r, d=d, p=p: e.matmul(
                            p[:, 0:128], wqkv[:, 3088 * d + 128 * r:3088 * d + 128 * (r + 1)],
                            xlnT[:, 128 * d:128 * (d + 1)], start=(d == 0), stop=(d == 7)),
                            reads=[wqkv, xlnT], writes=[p], inc=(d == 7))
                    k.run("dve", lambda e, r=r, p=p: e.tensor_copy(out=cvi[:, 131 * r + 3:131 * r + 131],
                                                                   in_=p[:, 0:128]),
                          reads=[p], writes=[cvi])
                for bi, (dst, c0) in enumerate(((bF, 3072), (aF, 3080))):
                    p = pj[bi % 2]
                    for d in range(8):
                        k.run("pe", lambda e, d=d, c0=c0, p=p: e.matmul(
                            p[0:8, 128:256], wqkv[:, 3088 * d + c0:3088 * d + c0 + 8],
                            xlnT[:, 128 * d:128 * (d + 1)], start=(d == 0), stop=(d == 7)),
                            reads=[wqkv, xlnT], writes=[p], inc=(d == 7))
                    k.run("dve", lambda e, dst=dst, p=p: e.tensor_copy(out=dst.ap, in_=p[0:8, 128:256]),
                          reads=[p], writes=[dst])
                if main:
                    for nzi in range(2):
                        p = pj[nzi % 2]
                        for d in range(8):
                            k.run("pe", lambda e, d=d, nzi=nzi, p=p: e.matmul(
                                p[:, 0:512], xlnT[:, 128 * d:128 * (d + 1)],
                                wz[:, 1024 * d + 512 * nzi:1024 * d + 512 * (nzi + 1)],
                                start=(d == 0), stop=(d == 7)),
                                reads=[wz, xlnT], writes=[p], inc=(d == 7))
                        k.run("act", lambda e, nzi=nzi, p=p: e.activation(
                            out=zsil[:, 512 * nzi:512 * (nzi + 1)], in_=p[:, 0:512], func=AF.Silu),
                            reads=[p], writes=[zsil])
                if b == 0:
                    if main:
                        k.dma("xl", xctx[0:3, :], x_d[0:3, :], writes=[xctx])
                        k.run("dve", lambda e: e.tensor_reduce(out=stat[0:3, 6:7], in_=xctx[0:3, :], axis=AXL.X,
                                                               op=ALU.add), reads=[xctx], writes=[stat])
                        k.run("dve", lambda e: e.tensor_scalar_mul(out=stat[0:3, 7:8], in0=stat[0:3, 6:7],
                                                                   scalar1=-1.0 / C), reads=[stat], writes=[stat])
                        k.run("act", lambda e: e.activation(out=xln[0:3, 0:C], in_=xctx[0:3, :], func=AF.Square,
                                                            bias=stat[0:3, 7:8], scale=cc[0:3, 2:3],
                                                            accum_out=stat[0:3, 6:7]),
                              reads=[xctx, stat, cc], writes=[xln, stat])
                        k.run("act", lambda e: e.activation(out=stat[0:3, 6:7], in_=stat[0:3, 6:7], func=AF.Sqrt,
                                                            bias=cc[0:3, 0:1], scale=cc[0:3, 5:6]),
                              reads=[stat, cc], writes=[stat])
                        k.run("dve", lambda e: e.reciprocal(out=stat[0:3, 6:7], in_=stat[0:3, 6:7]),
                              reads=[stat], writes=[stat])
                        k.run("dve", lambda e: e.tensor_tensor(out=stat[0:3, 7:8], in0=stat[0:3, 7:8],
                                                               in1=stat[0:3, 6:7], op=ALU.mult),
                              reads=[stat], writes=[stat])
                        k.run("act", lambda e: e.activation(out=xln[0:3, 0:C], in_=xctx[0:3, :], func=AF.Identity,
                                                            bias=stat[0:3, 7:8], scale=stat[0:3, 6:7]),
                              reads=[xctx, stat], writes=[xln])
                        for j in range(8):
                            k.run("pe", lambda e, j=j: e.transpose(pstb[0:128, 0:3],
                                                                   xln[0:3, 128 * j:128 * (j + 1)], idb[0:3, 0:3]),
                                  reads=[xln, idb], writes=[pstb])
                            k.run("dve", lambda e, j=j: e.tensor_copy(out=xctxT[:, 4 * j:4 * j + 3],
                                                                      in_=pstb[0:128, 0:3]),
                                  reads=[pstb], writes=[xctxT])
                        for r in range(24):
                            for d in range(8):
                                k.run("pe", lambda e, r=r, d=d: e.matmul(
                                    pj[0][0:128, 4:7], wqkv[:, 3088 * d + 128 * r:3088 * d + 128 * (r + 1)],
                                    xctxT[:, 4 * d:4 * d + 3], start=(d == 0), stop=(d == 7)),
                                    reads=[wqkv, xctxT], writes=[pj[0]], inc=(d == 7))
                            k.run("dve", lambda e, r=r: e.tensor_copy(out=cvi[:, 131 * r:131 * r + 3],
                                                                      in_=pj[0][0:128, 4:7]),
                                  reads=[pj[0]], writes=[cvi])
                    else:
                        k.run("dve", lambda e: e.memset(
                            cvi.h[:].rearrange("p (a b) -> p a b", b=131)[:, :, 0:3], 0.0), writes=[cvi])
                else:
                    k.run("dve", lambda e: e.tensor_copy(
                        out=cvi.h[:].rearrange("p (a b) -> p a b", b=131)[:, :, 0:3],
                        in_=cvprev.h[:].rearrange("p (a b) -> p a b", b=131)[:, :, 128:131]),
                        reads=[cvprev], writes=[cvi])
                for ri, r in enumerate(rows):
                    p = pj[ri % 2]
                    for j in range(4):
                        k.run("pe", lambda e, r=r, j=j, p=p: e.matmul(
                            p[:, 0:128], convd[:, 128 * (4 * r + j):128 * (4 * r + j + 1)],
                            cvi[:, 131 * r + j:131 * r + j + 128], start=(j == 0), stop=(j == 3)),
                            reads=[convd, cvi], writes=[p], inc=(j == 3))
                    k.run("act", lambda e, r=r, p=p: e.activation(out=qkvF[:, 128 * r:128 * (r + 1)],
                                                                  in_=p[:, 0:128], func=AF.Silu),
                          reads=[p], writes=[qkvF])
                r0, nr = rows[0], len(rows)
                k.dma("tq",
                      qkvT.h[:, 128 * r0:128 * (r0 + nr)].rearrange("p (a b) -> p a b", b=128),
                      qkvF.h[:, 128 * r0:128 * (r0 + nr)].rearrange("p (a b) -> p a b", b=128),
                      reads=[qkvF], writes=[qkvT], transpose=True)
                k.run("act", lambda e: e.activation(out=tmp8.ap, in_=aF.ap, func=AF.Exp,
                                                    bias=dtb.ap, scale=c8[:, 0:1]),
                      reads=[aF, dtb, c8], writes=[tmp8])
                k.run("act", lambda e: e.activation(out=gF.ap, in_=tmp8.ap, func=AF.Ln,
                                                    bias=c8[:, 0:1], scale=c8[:, 0:1]),
                      reads=[tmp8, c8], writes=[gF])
                k.run("dve", lambda e: e.tensor_scalar(out=gF.ap, in0=gF.ap, scalar1=nea.ap, scalar2=1.0,
                                                       op0=ALU.mult, op1=ALU.mult), reads=[gF, nea], writes=[gF])
                k.run("act", lambda e: e.activation(out=tmp8.ap, in_=bF.ap, func=AF.Exp,
                                                    bias=c8[:, 3:4], scale=c8[:, 2:3]),
                      reads=[bF, c8], writes=[tmp8])
                k.run("act", lambda e: e.activation(out=spbF.ap, in_=tmp8.ap, func=AF.Ln,
                                                    bias=c8[:, 0:1], scale=c8[:, 0:1]),
                      reads=[tmp8, c8], writes=[spbF])
                k.run("dve", lambda e: e.tensor_tensor_scan(out=gcF.ap, data0=ones8.ap, data1=gF.ap,
                                                            initial=0.0, op0=ALU.mult, op1=ALU.add),
                      reads=[ones8, gF], writes=[gcF])
                k.run("dve", lambda e: e.tensor_scalar_mul(out=ngcF.ap, in0=gcF.ap, scalar1=-1.0),
                      reads=[gcF], writes=[ngcF])
                for h in range(8):
                    if main:
                        k.run("act", lambda e, h=h: e.activation(out=junk[:, 0:128],
                                                                 in_=qkvT[:, 128 * h:128 * (h + 1)],
                                                                 func=AF.Square, accum_out=ssq[:, 8 + h:9 + h]),
                              reads=[qkvT], writes=[junk, ssq])
                    k.run("act", lambda e, h=h: e.activation(out=junk[:, 0:128],
                                                             in_=qkvT[:, 128 * (8 + h):128 * (9 + h)],
                                                             func=AF.Square, accum_out=ssq[:, h:h + 1]),
                          reads=[qkvT], writes=[junk, ssq])
                k.run("act", lambda e: e.activation(out=lss.ap, in_=ssq.ap, func=AF.Ln,
                                                    bias=cc[:, 1:2], scale=cc[:, 2:3]),
                      reads=[ssq, cc], writes=[lss])
                k.run("pe", lambda e: e.transpose(pj[0][0:8, 0:128], lss[:, 0:8], idf.ap),
                      reads=[lss, idf], writes=[pj[0]])
                k.run("pe", lambda e: e.transpose(pj[0][0:8, 128:256], lss[:, 8:16], idf.ap),
                      reads=[lss, idf], writes=[pj[0]])
                k.run("dve", lambda e: e.tensor_copy(out=lskF.ap, in_=pj[0][0:8, 0:128]),
                      reads=[pj[0]], writes=[lskF])
                k.run("dve", lambda e: e.tensor_copy(out=lsqF.ap, in_=pj[0][0:8, 128:256]),
                      reads=[pj[0]], writes=[lsqF])
                k.run("dve", lambda e: e.scalar_tensor_tensor(out=E1t.ap, in0=lskF.ap, scalar=-0.5,
                                                              in1=gcF.ap, op0=ALU.mult, op1=ALU.add),
                      reads=[lskF, gcF], writes=[E1t])
                k.run("dve", lambda e: e.tensor_tensor(out=E1t.ap, in0=E1t.ap, in1=spbF.ap,
                                                       op=ALU.subtract), reads=[E1t, spbF], writes=[E1t])
                k.run("dve", lambda e: e.scalar_tensor_tensor(out=E2t.ap, in0=lskF.ap, scalar=-0.5,
                                                              in1=ngcF.ap, op0=ALU.mult, op1=ALU.add),
                      reads=[lskF, ngcF], writes=[E2t])
                if main:
                    k.run("dve", lambda e: e.scalar_tensor_tensor(out=E3t.ap, in0=lsqF.ap, scalar=-0.5,
                                                                  in1=gcF.ap, op0=ALU.mult, op1=ALU.add),
                          reads=[lsqF, gcF], writes=[E3t])
                plans = [(E1t, "plain", 0), (E3t, "s", 8), (E2t, "gl", 16), (None, "glonly", 24), (spbF, "neg", 32)]
                for srct, mode, col in plans:
                    if mode == "gl":
                        k.run("act", lambda e, s=srct: e.activation(out=expt.ap, in_=s.ap, func=AF.Exp,
                                                                    bias=gcF[:, 127:128], scale=c8[:, 0:1]),
                              reads=[srct, gcF, c8], writes=[expt])
                    elif mode == "glonly":
                        k.run("act", lambda e: e.activation(out=expt.ap, in_=ones8.ap, func=AF.Exp,
                                                            bias=gcF[:, 127:128], scale=c8[:, 3:4]),
                              reads=[ones8, gcF, c8], writes=[expt])
                    elif mode == "neg":
                        k.run("act", lambda e, s=srct: e.activation(out=expt.ap, in_=s.ap, func=AF.Exp,
                                                                    bias=c8[:, 3:4], scale=c8[:, 2:3]),
                              reads=[srct, c8], writes=[expt])
                    elif mode == "s":
                        k.run("act", lambda e, s=srct: e.activation(out=expt.ap, in_=s.ap, func=AF.Exp,
                                                                    bias=c8[:, 1:2], scale=c8[:, 0:1]),
                              reads=[srct, c8], writes=[expt])
                    else:
                        k.run("act", lambda e, s=srct: e.activation(out=expt.ap, in_=s.ap, func=AF.Exp,
                                                                    bias=c8[:, 3:4], scale=c8[:, 0:1]),
                              reads=[srct, c8], writes=[expt])
                    k.run("pe", lambda e, col=col: e.transpose(pj[1][0:128, 256 + ((col // 8) % 2) * 8:256 + ((col // 8) % 2) * 8 + 8], expt.ap, idf[0:8, 0:8]),
                          reads=[expt, idf], writes=[pj[1]])
                    k.run("dve", lambda e, col=col: e.tensor_copy(out=tmcols[:, col:col + 8],
                                                                  in_=pj[1][0:128, 256 + ((col // 8) % 2) * 8:256 + ((col // 8) % 2) * 8 + 8]),
                          reads=[pj[1]], writes=[tmcols])
                for h in range(8):
                    k.run("dve", lambda e, h=h: e.tensor_scalar(out=vbeta[:, 128 * h:128 * (h + 1)],
                                                                in0=qkvT[:, 128 * (16 + h):128 * (17 + h)],
                                                                scalar1=tmcols[:, 32 + h:33 + h], scalar2=1.0,
                                                                op0=ALU.mult, op1=ALU.mult),
                          reads=[qkvT, tmcols], writes=[vbeta])
                    k.run("dve", lambda e, h=h: e.tensor_scalar(out=kprime[:, 128 * h:128 * (h + 1)],
                                                                in0=qkvT[:, 128 * (8 + h):128 * (9 + h)],
                                                                scalar1=tmcols[:, 16 + h:17 + h], scalar2=1.0,
                                                                op0=ALU.mult, op1=ALU.mult),
                          reads=[qkvT, tmcols], writes=[kprime])

            def scan_block(x_d, b, main):
                xt = xtile[b % 2]
                qkvF, qkvT = qkvF_r[b % 2], qkvT_r[b % 2]
                vbeta, kprime, zsil = vbeta_r[b % 2], kprime_r[b % 2], zsil_r[b % 2]
                sr = sring[b % 2]
                E1t, E2t, E3t, tmcols = sr["E1t"], sr["E2t"], sr["E3t"], sr["tmcols"]
                outb = outb_r[b % 2]
                def head_steps(h, main):
                    par = h % 4
                    T = tset[par]
                    P = (pc1, pc2, pcv[0], pst)[par]
                    PBsl = pstb[:, 128 * par:128 * (par + 1)]
                    M1_, M2_, B0_, C0_ = T["M1"], T["M2"], T["B0"], T["C0"]
                    attnT_, Bk_, Ck_, Wch_, vnref = T["attnT"], T["Bk"], T["Ck"], T["Wch"], T["vnref"]
                    Ssl = Sh[h]
                    kf = qkvF[:, 128 * (8 + h):128 * (9 + h)]
                    qf = qkvF[:, 128 * h:128 * (h + 1)]
                    A, B, Cs, D = P[:, 0:128], P[:, 128:256], P[:, 256:384], P[:, 384:512]
                    st = []
                    st.append(lambda: k.run("pe", lambda e: e.matmul(A, kf, kf, start=True, stop=True),
                                            reads=[qkvF], writes=[P]))
                    if main:
                        st.append(lambda: k.run("pe", lambda e: e.matmul(B, kf, qf, start=True, stop=True),
                                                reads=[qkvF], writes=[P]))
                    def m1grp():
                        k.run("pe", lambda e: e.matmul(Cs, sel[:, 128 * h:128 * (h + 1)], E1t.ap,
                                                       start=True, stop=False),
                              reads=[sel, E1t], writes=[P], inc=False)
                        k.run("pe", lambda e: e.matmul(Cs, E2t.ap, sel[:, 128 * h:128 * (h + 1)],
                                                       start=False, stop=False),
                              reads=[sel, E2t], writes=[P], inc=False)
                        k.run("pe", lambda e: e.matmul(Cs, idb.ap, lmS.ap, start=False, stop=True),
                              reads=[idb, lmS], writes=[P])
                    st.append(m1grp)
                    st.append(lambda: k.run("act", lambda e: e.activation(out=M1_.ap, in_=Cs, func=AF.Exp),
                                            reads=[P], writes=[M1_]))
                    st.append(lambda: k.run("dve", lambda e: e.tensor_tensor(out=B0_.ap, in0=A, in1=M1_.ap,
                                                                             op=ALU.mult),
                                            reads=[P, M1_], writes=[B0_]))
                    if main:
                        def m2grp():
                            k.run("pe", lambda e: e.matmul(Cs, E2t.ap, sel[:, 128 * h:128 * (h + 1)],
                                                           start=True, stop=False),
                                  reads=[sel, E2t], writes=[P], inc=False)
                            k.run("pe", lambda e: e.matmul(Cs, sel[:, 128 * h:128 * (h + 1)], E3t.ap,
                                                           start=False, stop=False),
                                  reads=[sel, E3t], writes=[P], inc=False)
                            k.run("pe", lambda e: e.matmul(Cs, idb.ap, lmI.ap, start=False, stop=True),
                                  reads=[idb, lmI], writes=[P])
                        st.append(m2grp)
                        st.append(lambda: k.run("act", lambda e: e.activation(out=M2_.ap, in_=Cs, func=AF.Exp),
                                                reads=[P], writes=[M2_]))
                        st.append(lambda: k.run("dve", lambda e: e.tensor_tensor(out=attnT_.ap, in0=B,
                                                                                 in1=M2_.ap, op=ALU.mult),
                                                reads=[P, M2_], writes=[attnT_]))
                    st.append(lambda: k.run("pe", lambda e: e.transpose(PBsl, B0_.ap, idb.ap),
                                            reads=[B0_, idb], writes=[pstb]))
                    st.append(lambda: k.run("dve", lambda e: e.tensor_copy(out=C0_.ap, in_=PBsl),
                                            reads=[pstb], writes=[C0_]))
                    prevB, prevC = [B0_], [C0_]

                    def mkpow(lev, sl, lhs, rhs, dst, eng):
                        def f():
                            a_, b_ = lhs[0], rhs[0]
                            k.run("pe", lambda e, a_=a_, b_=b_: e.matmul(sl, a_.ap, b_.ap, start=True, stop=True),
                                  reads=[a_, b_], writes=[P])
                            if eng == "dve":
                                k.run("dve", lambda e: e.tensor_copy(out=dst.ap, in_=sl), reads=[P], writes=[dst])
                            else:
                                k.run("act", lambda e: e.activation(out=dst.ap, in_=sl, func=AF.Copy),
                                      reads=[P], writes=[dst])
                        return f
                    for lev in range(6):
                        st.append(mkpow(lev, Cs, prevC, prevB, Bk_[lev], "dve" if lev % 2 == 0 else "act"))
                        if lev < 5:
                            st.append(mkpow(lev, B, prevB, prevC, Ck_[lev % 2], "act" if lev % 2 == 0 else "dve"))

                            def upd(lev=lev):
                                prevB[0] = Bk_[lev]
                                prevC[0] = Ck_[lev % 2]
                            st.append(upd)
                    def w0():
                        k.run("pe", lambda e: e.matmul(D, kf, Ssl.ap, start=True, stop=True),
                              reads=[qkvF, Ssl], writes=[P])
                        k.run("dve", lambda e: e.scalar_tensor_tensor(
                            out=Wch_[0].ap, in0=D, scalar=tmcols[:, h:h + 1],
                            in1=vbeta[:, 128 * h:128 * (h + 1)], op0=ALU.mult, op1=ALU.subtract),
                            reads=[P, tmcols, vbeta], writes=[Wch_[0]])
                    st.append(w0)
                    def slv0():
                        k.run("pe", lambda e: e.matmul(D, B0_.ap, Wch_[0].ap, start=True, stop=True),
                              reads=[B0_, Wch_[0]], writes=[P])
                        k.run("dve", lambda e: e.tensor_tensor(out=Wch_[1].ap, in0=Wch_[0].ap, in1=D,
                                                               op=ALU.subtract),
                              reads=[Wch_[0], P], writes=[Wch_[1]])
                    st.append(slv0)
                    cur = [1]
                    for lev in range(6):
                        def slv(lev=lev):
                            wsrc = Wch_[cur[0]]
                            if lev % 2 == 0:
                                k.run("pe", lambda e, w=wsrc: e.matmul(D, Bk_[lev].ap, w.ap,
                                                                       start=True, stop=True),
                                      reads=[Bk_[lev], wsrc], writes=[P])
                                if lev < 5:
                                    dst = Wch_[1 - cur[0]]
                                    k.run("dve", lambda e, w=wsrc, d=dst: e.tensor_tensor(
                                        out=d.ap, in0=w.ap, in1=D, op=ALU.add),
                                        reads=[wsrc, P], writes=[dst])
                                    cur[0] = 1 - cur[0]
                                else:
                                    pass
                            else:
                                k.run("pe", lambda e, w=wsrc: e.matmul(D, Bk_[lev].ap, w.ap,
                                                                       start=True, stop=False),
                                      reads=[Bk_[lev], wsrc], writes=[P], inc=False)
                                k.run("pe", lambda e, w=wsrc: e.matmul(D, idb.ap, w.ap,
                                                                       start=False, stop=True),
                                      reads=[idb, wsrc], writes=[P])
                                if lev < 5:
                                    dst = Wch_[1 - cur[0]]
                                    k.run("act", lambda e, d=dst: e.activation(out=d.ap, in_=D, func=AF.Copy),
                                          reads=[P], writes=[dst])
                                    cur[0] = 1 - cur[0]
                                else:
                                    vt = Wch_[1 - cur[0]]
                                    vnref[0] = vt
                                    k.run("act", lambda e, vt=vt: e.activation(out=vt.ap, in_=D, func=AF.Identity,
                                                                               scale=cc[:, 7:8]),
                                          reads=[P, cc], writes=[vt])
                        st.append(slv)
                    if main:
                        def outs():
                            vn_ = vnref[0]
                            k.run("pe", lambda e, vn_=vn_: e.matmul(Cs, attnT_.ap, vn_.ap, start=True, stop=True),
                                  reads=[attnT_, vn_], writes=[P])
                            k.run("dve", lambda e: e.tensor_copy(out=M2_.ap, in_=Cs), reads=[P], writes=[M2_])
                            k.run("pe", lambda e: e.matmul(D, qf, Ssl.ap, start=True, stop=True),
                                  reads=[qkvF, Ssl], writes=[P])
                            k.run("dve", lambda e: e.scalar_tensor_tensor(
                                out=outb[:, 128 * h:128 * (h + 1)], in0=D,
                                scalar=tmcols[:, 8 + h:9 + h], in1=M2_.ap, op0=ALU.mult, op1=ALU.add),
                                reads=[P, tmcols, M2_], writes=[outb])
                        st.append(outs)
                    def supd():
                        vn_ = vnref[0]
                        k.run("pe", lambda e, vn_=vn_: e.matmul(D, kprime[:, 128 * h:128 * (h + 1)], vn_.ap,
                                                                start=True, stop=True),
                              reads=[kprime, vn_], writes=[P])
                        k.run("dve", lambda e: e.scalar_tensor_tensor(
                            out=Ssl.ap, in0=Ssl.ap, scalar=tmcols[:, 24 + h:25 + h], in1=D,
                            op0=ALU.mult, op1=ALU.add),
                            reads=[Ssl, tmcols, P], writes=[Ssl])
                    st.append(supd)
                    return st

                pthunks = list(scan_block.pending)
                scan_block.pending = []
                psteps = max(1, len(pthunks) // 18) if pthunks else 0
                pi = 0
                for hp in range(2):
                    chains = [head_steps(4 * hp + c, main) for c in range(4)]
                    idxs = [0] * 4
                    while any(idxs[c] < len(chains[c]) for c in range(4)):
                        for c in range(4):
                            if idxs[c] < len(chains[c]):
                                chains[c][idxs[c]]()
                                idxs[c] += 1
                        for _ in range(psteps):
                            if pi < len(pthunks):
                                pthunks[pi]()
                                pi += 1
                while pi < len(pthunks):
                    pthunks[pi]()
                    pi += 1

            def norm_block(b):
                xt = xtile[b % 2]
                zsil = zsil_r[b % 2]
                outb = outb_r[b % 2]
                if True:
                    hgT = xln
                    hgate = outb
                    for h in range(8):
                        k.run("act", lambda e, h=h: e.activation(out=junk[:, 0:128],
                                                                 in_=outb[:, 128 * h:128 * (h + 1)],
                                                                 func=AF.Square, accum_out=rms[:, h:h + 1]),
                              reads=[outb], writes=[junk, rms])
                    k.run("act", lambda e: e.activation(out=rms.ap, in_=rms.ap, func=AF.Sqrt,
                                                        bias=cc[:, 1:2], scale=cc[:, 6:7]),
                          reads=[rms, cc], writes=[rms])
                    k.run("dve", lambda e: e.reciprocal(out=rms.ap, in_=rms.ap), reads=[rms], writes=[rms])
                    for h in range(8):
                        k.run("dve", lambda e, h=h: e.tensor_scalar(out=hgate[:, 128 * h:128 * (h + 1)],
                                                                    in0=outb[:, 128 * h:128 * (h + 1)],
                                                                    scalar1=rms[:, h:h + 1], scalar2=1.0,
                                                                    op0=ALU.mult, op1=ALU.mult),
                              reads=[outb, rms], writes=[hgate])
                    k.run("dve", lambda e: e.tensor_tensor(out=hgate.ap, in0=hgate.ap, in1=zsil.ap,
                                                           op=ALU.mult), reads=[hgate, zsil], writes=[hgate])
                    k.dma("tq", hgT.h[:].rearrange("p (a b) -> p a b", b=128),
                          hgate.h[:].rearrange("p (a b) -> p a b", b=128),
                          reads=[hgate], writes=[hgT], transpose=True)
                    for half in range(2):
                        p = pj[half]
                        for d in range(8):
                            k.run("pe", lambda e, d=d, half=half, p=p: e.matmul(
                                p[:, 0:512], hgT[:, 128 * d:128 * (d + 1)],
                                wout[:, 1024 * d + 512 * half:1024 * d + 512 * (half + 1)],
                                start=(d == 0), stop=(d == 7)),
                                reads=[hgT, wout], writes=[p], inc=(d == 7))
                        k.run("dve", lambda e, half=half, p=p, xt=xt: e.tensor_tensor(
                            out=xt[:, 512 * half:512 * (half + 1)], in0=p[:, 0:512],
                            in1=xt[:, 512 * half:512 * (half + 1)], op=ALU.add),
                            reads=[p, xt], writes=[xt])
                    k.dma("sp1", xmid_d[BLK * b:BLK * (b + 1), :], xt.ap, reads=[xt, xmid_t], writes=[xmid_t])

            scan_block.pending = []
            prep_block(xp_d, 0, main=False)
            for b in range(nb_pref):
                k.begin_defer()
                if b + 1 < nb_pref:
                    prep_block(xp_d, b + 1, main=False)
                else:
                    prep_block(xm_d, 0, main=True)
                scan_block.pending = k.end_defer()
                scan_block(xp_d, b, main=False)
            for b in range(nb_main):
                k.begin_defer()
                if b > 0:
                    norm_block(b - 1)
                if b + 1 < nb_main:
                    prep_block(xm_d, b + 1, main=True)
                scan_block.pending = k.end_defer()
                scan_block(xm_d, b, main=True)
            norm_block(nb_main - 1)
            k.barrier()

        with ExitStack() as fctx:
            def fsb(name, shape, dt):
                return Tile(fctx.enter_context(nc.sbuf_tensor(name, list(shape), dt)), name)

            b1row, b2row = bbrow, bbrow
            w1q = fsb("w1q_s", [128, 8 * 4096], F16)
            w2q = fsb("w2q_s", [128, 32 * 1024], F16)
            xmf = [fsb(f"xmf{i}", [128, C], F32) for i in range(2)]
            xc = fsb("xc", [128, C], F32)
            tq1 = fsb("tq1", [128, C], F32)
            qi1 = fsb("qi1", [128, C], I32)
            qf1 = xc
            q16a_r = [fsb(f"q16a{i}", [128, C], F16) for i in range(2)]
            q16aT_r = [fsb(f"q16aT{i}", [128, C], F16) for i in range(2)]
            g1 = fsb("g1", [128, HF], F32)
            QCH = 512
            tq2 = fsb("tq2", [128, QCH], F32)
            qi2 = fsb("qi2", [128, QCH], I32)
            qf2 = fsb("qf2", [128, QCH], F32)
            q16b = fsb("q16b", [128, QCH], F16)
            q16bT = fsb("q16bT", [128, HF], F16)
            invr_r = [fsb(f"invr{i}", [1, 128], BF16) for i in range(2)]
            statf_r = [fsb(f"statf{i}", [128, 8], F32) for i in range(2)]
            otile_r = [fsb(f"otile{i}", [128, C], F32) for i in range(2)]

            for t_ in (otile_r[0], otile_r[1], g1, xc, tq1, q16aT_r[0], q16aT_r[1], q16bT, q16a_r[0], q16a_r[1], tq2, q16b, invr_r[0], invr_r[1], statf_r[0], statf_r[1], qi1, qi2, xmf[0], xmf[1]):
                k.run("dve", lambda e, t_=t_: e.memset(t_.ap, 0.0), writes=[t_])
            k.dma("wl", w1q.ap, w1q_d, reads=[w1q_t], writes=[w1q])
            k.dma("wl", w2q.ap, w2q_d, reads=[w2q_t], writes=[w2q])

            def ffn_front(b):
                xt = xmf[b % 2]
                stat = statf_r[b % 2]
                invr = invr_r[b % 2]
                q16a, q16aT = q16a_r[b % 2], q16aT_r[b % 2]
                k.dma("xl", xt.ap, xmid_d[BLK * b:BLK * (b + 1), :], reads=[xmid_t], writes=[xt])
                k.run("dve", lambda e, xt=xt: e.tensor_reduce(out=stat[:, 0:1], in_=xt.ap, axis=AXL.X, op=ALU.add),
                      reads=[xt], writes=[stat])
                k.run("dve", lambda e: e.tensor_scalar_mul(out=stat[:, 1:2], in0=stat[:, 0:1], scalar1=-1.0 / C),
                      reads=[stat], writes=[stat])
                k.run("act", lambda e, xt=xt: e.activation(out=xc.ap, in_=xt.ap, func=AF.Identity,
                                                           bias=stat[:, 1:2], scale=cc[:, 2:3]),
                      reads=[xt, stat, cc], writes=[xc])
                k.run("act", lambda e: e.activation(out=tq1.ap, in_=xc.ap, func=AF.Square,
                                                    accum_out=stat[:, 2:3]), reads=[xc], writes=[tq1, stat])
                k.run("act", lambda e: e.activation(out=stat[:, 3:4], in_=stat[:, 2:3], func=AF.Sqrt,
                                                    bias=cc[:, 0:1], scale=cc[:, 5:6]),
                      reads=[stat, cc], writes=[stat])
                k.run("dve", lambda e: e.reciprocal(out=stat[:, 4:5], in_=stat[:, 3:4]), reads=[stat], writes=[stat])
                k.run("dve", lambda e: e.tensor_reduce(out=stat[:, 5:6], in_=xc.ap, axis=AXL.X, op=ALU.max,
                                                       apply_absolute_value=True), reads=[xc], writes=[stat])
                k.run("dve", lambda e: e.tensor_scalar(out=stat[:, 5:6], in0=stat[:, 5:6], scalar1=1e-30,
                                                       scalar2=1.0, op0=ALU.max, op1=ALU.mult),
                      reads=[stat], writes=[stat])
                k.run("dve", lambda e: e.reciprocal(out=stat[:, 6:7], in_=stat[:, 5:6]), reads=[stat], writes=[stat])
                k.run("dve", lambda e: e.tensor_scalar_mul(out=stat[:, 6:7], in0=stat[:, 6:7], scalar1=127.0),
                      reads=[stat], writes=[stat])
                k.run("act", lambda e: e.activation(out=tq1.ap, in_=xc.ap, func=AF.Identity,
                                                    bias=cc[:, 3:4], scale=stat[:, 6:7]),
                      reads=[xc, stat, cc], writes=[tq1])
                k.run("dve", lambda e: e.tensor_copy(out=qi1.ap, in_=tq1.ap), reads=[tq1], writes=[qi1])
                k.run("dve", lambda e: e.tensor_copy(out=qf1.ap, in_=qi1.ap), reads=[qi1], writes=[qf1])
                k.run("dve", lambda e: e.tensor_tensor(out=q16a.ap, in0=qf1.ap, in1=tq1.ap, op=ALU.is_gt),
                      reads=[qf1, tq1], writes=[q16a])
                k.run("dve", lambda e: e.scalar_tensor_tensor(out=q16a.ap, in0=qf1.ap, scalar=-256.0,
                                                              in1=q16a.ap, op0=ALU.add, op1=ALU.subtract),
                      reads=[qf1, q16a], writes=[q16a])
                k.dma("tq", q16aT.h[:].rearrange("p (a b) -> p a b", b=128),
                      q16a.h[:].rearrange("p (a b) -> p a b", b=128),
                      reads=[q16a], writes=[q16aT], transpose=True)
                k.run("dve", lambda e: e.tensor_tensor(out=stat[:, 7:8], in0=stat[:, 5:6], in1=stat[:, 4:5],
                                                       op=ALU.mult), reads=[stat], writes=[stat])
                k.run("dve", lambda e: e.tensor_scalar(out=stat[:, 7:8], in0=stat[:, 7:8], scalar1=1e-5,
                                                       scalar2=1.0 / 127.0, op0=ALU.max, op1=ALU.mult),
                      reads=[stat], writes=[stat])
                k.run("dve", lambda e: e.tensor_tensor(out=stat[:, 7:8], in0=stat[:, 7:8], in1=w1s[:, 0:1],
                                                       op=ALU.mult), reads=[stat, w1s], writes=[stat])
                k.run("dve", lambda e: e.reciprocal(out=stat[:, 0:1], in_=stat[:, 7:8]), reads=[stat], writes=[stat])
                k.run("pe", lambda e: e.transpose(pcv[0][0:1, 0:128], stat[:, 0:1], idf.ap),
                      reads=[stat, idf], writes=[pcv[0]])
                k.run("dve", lambda e: e.tensor_copy(out=invr.ap, in_=pcv[0][0:1, 0:128]),
                      reads=[pcv[0]], writes=[invr])
            def ffn_back(b):
                xt = xmf[b % 2]
                stat = statf_r[b % 2]
                invr = invr_r[b % 2]
                otile = otile_r[b % 2]
                q16aT = q16aT_r[b % 2]
                pthunks = list(ffn_back.pending)
                ffn_back.pending = []
                pstate = [0]
                def drain(n_):
                    e_ = min(pstate[0] + n_, len(pthunks))
                    while pstate[0] < e_:
                        pthunks[pstate[0]]()
                        pstate[0] += 1
                nper = -(-len(pthunks) // 10) if pthunks else 0
                for n in range(8):
                    p = pj[n % 2]
                    for d in range(8):
                        k.run("pe", lambda e, d=d, n=n, p=p: e.matmul(
                            p[:, 0:512], q16aT[:, 128 * d:128 * (d + 1)],
                            w1q[:, 4096 * d + 512 * n:4096 * d + 512 * (n + 1)],
                            start=(d == 0), stop=False),
                            reads=[q16aT, w1q], writes=[p], inc=False)
                    k.run("pe", lambda e, n=n, p=p: e.matmul(
                        p[:, 0:512], invr.ap, b1row[0:1, 512 * n:512 * (n + 1)], start=False, stop=True),
                        reads=[invr, b1row], writes=[p])
                    k.run("act", lambda e, n=n, p=p: e.activation(out=g1[:, 512 * n:512 * (n + 1)],
                                                                  in_=p[:, 0:512], func=AF.Gelu,
                                                                  scale=stat[:, 7:8]),
                          reads=[p, stat], writes=[g1])
                    drain(nper)
                k.run("dve", lambda e: e.tensor_reduce(out=stat[:, 5:6], in_=g1.ap, axis=AXL.X, op=ALU.max,
                                                       apply_absolute_value=True), reads=[g1], writes=[stat])
                k.run("dve", lambda e: e.tensor_scalar(out=stat[:, 5:6], in0=stat[:, 5:6], scalar1=1e-30,
                                                       scalar2=1.0, op0=ALU.max, op1=ALU.mult),
                      reads=[stat], writes=[stat])
                k.run("dve", lambda e: e.reciprocal(out=stat[:, 6:7], in_=stat[:, 5:6]), reads=[stat], writes=[stat])
                k.run("dve", lambda e: e.tensor_scalar_mul(out=stat[:, 6:7], in0=stat[:, 6:7], scalar1=127.0),
                      reads=[stat], writes=[stat])
                for qc in range(HF // QCH):
                    k.run("act", lambda e, qc=qc: e.activation(out=tq2.ap, in_=g1[:, QCH * qc:QCH * (qc + 1)],
                                                               func=AF.Identity, bias=cc[:, 3:4], scale=stat[:, 6:7]),
                          reads=[g1, stat, cc], writes=[tq2])
                    k.run("dve", lambda e: e.tensor_copy(out=qi2.ap, in_=tq2.ap), reads=[tq2], writes=[qi2])
                    k.run("dve", lambda e: e.tensor_copy(out=qf2[:, 0:QCH], in_=qi2.ap), reads=[qi2], writes=[qf2])
                    k.run("dve", lambda e: e.tensor_tensor(out=q16b.ap, in0=qf2[:, 0:QCH], in1=tq2.ap, op=ALU.is_gt),
                          reads=[qf2, tq2], writes=[q16b])
                    k.run("dve", lambda e: e.scalar_tensor_tensor(out=q16b.ap, in0=qf2[:, 0:QCH], scalar=-256.0,
                                                                  in1=q16b.ap, op0=ALU.add, op1=ALU.subtract),
                          reads=[qf2, q16b], writes=[q16b])
                    k.dma("tq",
                          q16bT.h[:, QCH * qc:QCH * (qc + 1)].rearrange("p (a b) -> p a b", b=128),
                          q16b.h[:].rearrange("p (a b) -> p a b", b=128),
                          reads=[q16b], writes=[q16bT], transpose=True)
                    drain(nper)
                drain(len(pthunks))
                k.run("dve", lambda e: e.tensor_scalar(out=stat[:, 7:8], in0=stat[:, 5:6], scalar1=1e-5,
                                                       scalar2=1.0 / 127.0, op0=ALU.max, op1=ALU.mult),
                      reads=[stat], writes=[stat])
                k.run("dve", lambda e: e.tensor_tensor(out=stat[:, 7:8], in0=stat[:, 7:8], in1=w2s[:, 0:1],
                                                       op=ALU.mult), reads=[stat, w2s], writes=[stat])
                k.run("dve", lambda e: e.reciprocal(out=stat[:, 0:1], in_=stat[:, 7:8]), reads=[stat], writes=[stat])
                k.run("pe", lambda e: e.transpose(pst[0:1, 0:128], stat[:, 0:1], idf.ap),
                      reads=[stat, idf], writes=[pst])
                k.run("dve", lambda e: e.tensor_copy(out=invr.ap, in_=pst[0:1, 0:128]),
                      reads=[pst], writes=[invr])
                for n in range(2):
                    p = pj[n % 2]
                    for d in range(32):
                        k.run("pe", lambda e, d=d, n=n, p=p: e.matmul(
                            p[:, 0:512], q16bT[:, 128 * d:128 * (d + 1)],
                            w2q[:, 1024 * d + 512 * n:1024 * d + 512 * (n + 1)],
                            start=(d == 0), stop=False),
                            reads=[q16bT, w2q], writes=[p], inc=False)
                    k.run("pe", lambda e, n=n, p=p: e.matmul(
                        p[:, 0:512], invr.ap, b2row[0:1, HF + 512 * n:HF + 512 * (n + 1)], start=False, stop=True),
                        reads=[invr, b2row], writes=[p])
                    k.run("dve", lambda e, n=n, p=p, xt=xt: e.scalar_tensor_tensor(
                        out=otile[:, 512 * n:512 * (n + 1)], in0=p[:, 0:512], scalar=stat[:, 7:8],
                        in1=xt[:, 512 * n:512 * (n + 1)], op0=ALU.mult, op1=ALU.add),
                        reads=[p, stat, xt], writes=[otile])
                k.dma("st", out_d[BLK * b:BLK * (b + 1), :], otile.ap, reads=[otile])

            ffn_back.pending = []
            ffn_front(0)
            for b in range(nb_main):
                k.begin_defer()
                if b + 1 < nb_main:
                    ffn_front(b + 1)
                ffn_back.pending = k.end_defer()
                ffn_back(b)
            k.barrier()
        k.finalize(block)
    return nc


_PROG_CACHE = {}


def kernel(**inputs):
    nb_main, nb_pref = 32, 32
    per_core = host_prep(inputs, nb_main, nb_pref)
    key = (nb_main, nb_pref)
    if key not in _PROG_CACHE:
        _PROG_CACHE[key] = build_program(nb_main, nb_pref)
    nc = _PROG_CACHE[key]
    res = run_bass_kernel_spmd(nc, per_core, core_ids=list(range(8)))
    out = np.empty((4, 8192, 1024), np.float32)
    for c in range(8):
        b, half = c // 2, c % 2
        out[b, half * 4096:(half + 1) * 4096] = res.results[c]["out"]
    return out


# version 11
# speedup vs baseline: 1.0221x; 1.0221x over previous
import numpy as np
import ml_dtypes
from contextlib import ExitStack

import concourse.bass as bass
import concourse.mybir as mybir
from concourse.bass_utils import run_bass_kernel_spmd

F32 = mybir.dt.float32
BF16 = mybir.dt.bfloat16
F16 = mybir.dt.float16
I32 = mybir.dt.int32
AF = mybir.ActivationFunctionType
ALU = mybir.AluOpType
AXL = mybir.AxisListType
BF = ml_dtypes.bfloat16

C = 1024
NH = 8
BLK = 128
HF = 4096
LOGM = -1.0e5
LN128_HALF = 0.5 * float(np.log(128.0))


class Tile:
    __slots__ = ("h", "writer", "readers", "name")

    def __init__(self, h, name):
        self.h = h
        self.name = name
        self.writer = None
        self.readers = {}

    def __getitem__(self, idx):
        return self.h[idx]

    @property
    def ap(self):
        return self.h[:]


class Ker:
    ENG = {"sp": "sync", "pe": "tensor", "act": "scalar", "dve": "vector", "pool": "gpsimd"}

    def __init__(self, nc, ctx):
        self.nc = nc
        self.ctx = ctx
        self.sems = {}
        self.counts = {}
        self.watermark = {}
        self.tape = {e: [] for e in self.ENG}
        for e in self.ENG:
            self._mksem("s_" + e)
        self.streams = {}

    def _mksem(self, name):
        self.sems[name] = self.ctx.enter_context(self.nc.semaphore(name))
        self.counts[name] = 0

    def stream(self, name, nslots=3):
        slots = []
        for i in range(nslots):
            sn = f"d_{name}{i}"
            self._mksem(sn)
            slots.append(sn)
        self.streams[name] = [slots, 0]

    def sb(self, name, shape, dtype):
        return Tile(self.ctx.enter_context(self.nc.sbuf_tensor(name, list(shape), dtype)), name)

    def ps(self, name, shape, dtype=F32):
        return Tile(self.ctx.enter_context(self.nc.psum_tensor(name, list(shape), dtype)), name)

    def _dep_waits(self, ename, reads, writes):
        waits = []

        def add(tok):
            if tok is None:
                return
            sname, val = tok
            if val <= 0:
                return
            key = (ename, sname)
            if self.watermark.get(key, 0) >= val:
                return
            self.watermark[key] = val
            waits.append((sname, val))
        for t in reads:
            add(t.writer)
        for t in writes:
            add(t.writer)
            for sn, val in t.readers.items():
                add((sn, val))
        return waits

    def begin_defer(self):
        self._defer = []

    def end_defer(self):
        d = self._defer
        self._defer = None
        return d

    def run(self, ename, call, reads=(), writes=(), inc=True):
        if getattr(self, "_defer", None) is not None:
            self._defer.append(lambda ename=ename, call=call, reads=tuple(reads), writes=tuple(writes), inc=inc:
                               self._run_now(ename, call, reads, writes, inc))
            return None
        return self._run_now(ename, call, reads, writes, inc)

    def _run_now(self, ename, call, reads=(), writes=(), inc=True):
        waits = self._dep_waits(ename, reads, writes)
        sname = "s_" + ename
        if inc:
            self.counts[sname] += 1
            incv = (self.sems[sname], 1)
        else:
            incv = None
        self.tape[ename].append((waits, call, incv))
        tok = (sname, self.counts[sname])
        for t in writes:
            t.writer = tok
            t.readers = {}
        for t in reads:
            if t.readers.get(tok[0], 0) < tok[1]:
                t.readers[tok[0]] = tok[1]
        return tok

    def dma(self, stream, out_ap, in_ap, reads=(), writes=(), ename="sp", transpose=False):
        if getattr(self, "_defer", None) is not None:
            self._defer.append(lambda stream=stream, out_ap=out_ap, in_ap=in_ap, reads=tuple(reads),
                               writes=tuple(writes), ename=ename, transpose=transpose:
                               self._dma_now(stream, out_ap, in_ap, reads, writes, ename, transpose))
            return None
        return self._dma_now(stream, out_ap, in_ap, reads, writes, ename, transpose)

    def _dma_now(self, stream, out_ap, in_ap, reads=(), writes=(), ename="sp", transpose=False):
        waits = self._dep_waits(ename, reads, writes)
        slots, idx = self.streams[stream]
        sname = slots[idx % len(slots)]
        self.streams[stream][1] = idx + 1
        prior = self.counts[sname]
        if prior > 0 and self.watermark.get((ename, sname), 0) < prior:
            self.watermark[(ename, sname)] = prior
            waits.append((sname, prior))
        self.counts[sname] += 16

        def call(eng, o=out_ap, i=in_ap, tr=transpose):
            if tr:
                return eng.dma_start_transpose(out=o, in_=i)
            return eng.dma_start(out=o, in_=i)
        self.tape[ename].append((waits, call, (self.sems[sname], 16)))
        tok = (sname, self.counts[sname])
        for t in writes:
            t.writer = tok
            t.readers = {}
        for t in reads:
            if t.readers.get(tok[0], 0) < tok[1]:
                t.readers[tok[0]] = tok[1]
        return tok

    def barrier(self):
        snap = dict(self.counts)
        for e in self.ENG:
            waits = []
            for sname, val in snap.items():
                if val <= 0:
                    continue
                key = (e, sname)
                if self.watermark.get(key, 0) >= val:
                    continue
                self.watermark[key] = val
                waits.append((sname, val))
            if waits:
                self.tape[e].append((waits, None, None))

    def finalize(self, block):
        for ename, attr in self.ENG.items():
            entries = self.tape[ename]
            deco = getattr(block, attr)

            def body(eng, entries=entries):
                for waits, call, incv in entries:
                    for sname, val in waits:
                        eng.wait_ge(self.sems[sname], val)
                    if call is not None:
                        inst = call(eng)
                        if incv is not None:
                            inst.then_inc(*incv)
            deco(body)


def host_prep(inputs, nb_main, nb_pref):
    x = np.ascontiguousarray(np.asarray(inputs["L0_attn_x"], dtype=np.float32))
    Wqkvz = np.asarray(inputs["L0_attn_W_qkvz"], dtype=np.float32)
    Wba = np.asarray(inputs["L0_attn_W_ba"], dtype=np.float32)
    convw = np.asarray(inputs["L0_attn_conv_w"], dtype=np.float32)[:, 0, :]
    dtb = np.asarray(inputs["L0_attn_dt_bias"], dtype=np.float32)
    alog = np.asarray(inputs["L0_attn_A_log"], dtype=np.float32)
    nw = np.asarray(inputs["L0_attn_norm_weight"], dtype=np.float32)
    Wout = np.asarray(inputs["L0_attn_W_out"], dtype=np.float32)
    W1 = np.asarray(inputs["L0_ffn_W1"], dtype=np.float32)
    b1 = np.asarray(inputs["L0_ffn_b1"], dtype=np.float32)
    W2 = np.asarray(inputs["L0_ffn_W2"], dtype=np.float32)
    b2 = np.asarray(inputs["L0_ffn_b2"], dtype=np.float32)

    cols = [Wqkvz[:, 512 * h + off:512 * h + off + 128] for off in (0, 128, 256) for h in range(NH)]
    qkv = np.concatenate(cols, axis=1)
    ba = np.concatenate([Wba[:, 0::2], Wba[:, 1::2]], axis=1)
    wqkv = np.concatenate([qkv, ba], axis=1)
    wqkv = np.ascontiguousarray(wqkv.reshape(8, 128, 3088).transpose(1, 0, 2).reshape(128, 8 * 3088))
    wz = np.concatenate([Wqkvz[:, 512 * h + 384:512 * h + 512] for h in range(NH)], axis=1)
    wz = np.ascontiguousarray(wz.reshape(8, 128, 1024).transpose(1, 0, 2).reshape(128, 8 * 1024))
    convw_r = np.ascontiguousarray(convw.reshape(24, 128, 4).transpose(1, 0, 2).reshape(128, 96))
    wout_r = np.ascontiguousarray(Wout.reshape(8, 128, 1024).transpose(1, 0, 2).reshape(128, 8 * 1024))
    w1t = np.ascontiguousarray(W1.T.reshape(8, 128, 4096).transpose(1, 0, 2).reshape(128, 8 * 4096))
    w2t = np.ascontiguousarray(W2.T.reshape(32, 128, 1024).transpose(1, 0, 2).reshape(128, 32 * 1024))

    ar = np.arange(128)
    lmS = np.where(ar[None, :] > ar[:, None], 0.0, LOGM).astype(np.float32)
    lmI = np.where(ar[None, :] >= ar[:, None], -LN128_HALF, LOGM).astype(np.float32)
    sel = np.zeros((8, 1024), np.float32)
    for r in range(8):
        sel[r, 128 * r:128 * (r + 1)] = 1.0

    shared = dict(
        wqkv=wqkv, wz=wz, convw=convw_r, wout=wout_r,
        w1t=w1t, b1=b1.reshape(1, HF), w2t=w2t, b2=b2.reshape(1, C),
        dtb=dtb.reshape(8, 1), alog=alog.reshape(8, 1), nwc=nw.reshape(128, 1),
        idf=np.eye(128, dtype=np.float32), idb=np.eye(128).astype(BF),
        lmS=lmS.astype(BF), lmI=lmI.astype(BF), sel=sel,
    )
    Tm, Tp = nb_main * BLK, nb_pref * BLK
    per_core = []
    for b in range(x.shape[0]):
        for half in range(2):
            t0 = half * Tm
            xm = np.zeros((Tm + 3, C), np.float32)
            xm[3:] = x[b, t0:t0 + Tm]
            if half == 1:
                xm[:3] = x[b, t0 - 3:t0]
            xp = np.zeros((Tp + 3, C), np.float32)
            if half == 1:
                xp[3:] = x[b, 0:Tp]
            d = dict(shared)
            d["xm"] = xm
            d["xp"] = xp
            per_core.append(d)
    return per_core


def build_program(nb_main, nb_pref):
    nc = bass.Bass("TRN2", target_bir_lowering=False)
    Tm, Tp = nb_main * BLK, nb_pref * BLK

    def din(name, shape, dt=F32):
        return nc.dram_tensor(name, list(shape), dt, kind="ExternalInput").ap()

    xm_d = din("xm", [Tm + 3, C])
    xp_d = din("xp", [Tp + 3, C])
    wqkv_d = din("wqkv", [128, 8 * 3088])
    wz_d = din("wz", [128, 8 * 1024])
    convw_d = din("convw", [128, 96])
    wout_d = din("wout", [128, 8 * 1024])
    w1t_d = din("w1t", [128, 8 * 4096])
    b1_d = din("b1", [1, HF])
    w2t_d = din("w2t", [128, 32 * 1024])
    b2_d = din("b2", [1, C])
    dtb_d = din("dtb", [8, 1])
    alog_d = din("alog", [8, 1])
    nwc_d = din("nwc", [128, 1])
    idf_d = din("idf", [128, 128])
    idb_d = din("idb", [128, 128], BF16)
    lmS_d = din("lmS", [128, 128], BF16)
    lmI_d = din("lmI", [128, 128], BF16)
    sel_d = din("sel", [8, 1024])
    out_d = nc.dram_tensor("out", [Tm, C], F32, kind="ExternalOutput").ap()
    xmid_d = nc.dram_tensor("xmid", [Tm, C], F32).ap()
    w1q_d = nc.dram_tensor("w1q", [128, 8 * 4096], F16).ap()
    w2q_d = nc.dram_tensor("w2q", [128, 32 * 1024], F16).ap()

    ctx = ExitStack()
    with ctx:
        k = Ker(nc, ctx)
        xmid_t = Tile(None, "xmid_t")
        w1q_t = Tile(None, "w1q_t")
        w2q_t = Tile(None, "w2q_t")
        for s, n in [("wl", 2), ("xl", 3), ("tq", 4), ("sp1", 2), ("st", 2), ("cw", 5)]:
            k.stream(s, n)

        idf = k.sb("idf_s", [128, 128], F32)
        dtb = k.sb("dtb_s", [8, 1], F32)
        alog = k.sb("alog_s", [8, 1], F32)
        nea = k.sb("nea", [8, 1], F32)
        nwc = k.sb("nwc_s", [128, 1], F32)
        ones8 = k.sb("ones8", [8, 128], F32)
        onesc = k.sb("onesc", [128, 1], F32)
        onesr = k.sb("onesr", [1, 128], F32)
        cc = k.sb("cc", [128, 8], F32)
        c8 = k.sb("c8", [8, 4], F32)
        w1s = k.sb("w1s", [128, 2], F32)
        w2s = k.sb("w2s", [128, 2], F32)
        bbrow = k.sb("bbrow", [1, HF + C], BF16)
        stat = k.sb("stat", [128, 8], F32)
        junk = k.sb("junk", [128, 128], BF16)
        junk2 = k.sb("junk2", [1, 4], F32)
        pj = [k.ps(f"pj{i}", [128, 512]) for i in range(2)]
        pcv0 = k.ps("pcv0", [128, 512])
        pcv = [pcv0, pcv0]
        pc1 = k.ps("pc1", [128, 512])
        pc2 = k.ps("pc2", [128, 512])
        pst = k.ps("pst", [128, 512])
        pstb = k.ps("pstb", [128, 512], BF16)

        block = ctx.enter_context(nc.Block(no_gpsimd_drain=True))

        k.dma("wl", idf.ap, idf_d, writes=[idf])
        k.dma("wl", dtb.ap, dtb_d, writes=[dtb])
        k.dma("wl", alog.ap, alog_d, writes=[alog])
        k.dma("wl", nwc.ap, nwc_d, writes=[nwc])
        k.run("dve", lambda e: e.memset(ones8.ap, 1.0), writes=[ones8])
        k.run("dve", lambda e: e.memset(onesc.ap, 1.0), writes=[onesc])
        k.run("dve", lambda e: e.memset(onesr.ap, 1.0), writes=[onesr])
        k.dma("cw", bbrow[0:1, 0:HF], b1_d, writes=[bbrow], ename="pool")
        k.dma("cw", bbrow[0:1, HF:HF + C], b2_d, writes=[bbrow], ename="pool")
        for col, val in [(0, 1e-5), (1, 1e-6), (2, 1.0), (3, 256.5), (4, 0.0), (5, 1.0 / C), (6, 1.0 / 128.0), (7, -1.0)]:
            k.run("dve", lambda e, col=col, val=val: e.memset(cc[:, col:col + 1], val), writes=[cc])
        for col, val in [(0, 1.0), (1, -LN128_HALF), (2, -1.0), (3, 0.0)]:
            k.run("dve", lambda e, col=col, val=val: e.memset(c8[:, col:col + 1], val), writes=[c8])
        k.run("act", lambda e: e.activation(out=nea.ap, in_=alog.ap, func=AF.Exp), reads=[alog], writes=[nea])
        k.run("dve", lambda e: e.tensor_scalar_mul(out=nea.ap, in0=nea.ap, scalar1=-1.0), reads=[nea], writes=[nea])

        with ExitStack() as tctx:
            CH = 8192
            stg = Tile(tctx.enter_context(nc.sbuf_tensor("stg", [128, CH], F32)), "stg")
            gp_ = Tile(tctx.enter_context(nc.sbuf_tensor("gp_", [128, CH], BF16)), "gp_")
            gn_ = Tile(tctx.enter_context(nc.sbuf_tensor("gn_", [128, CH], BF16)), "gn_")
            qw = Tile(tctx.enter_context(nc.sbuf_tensor("qw", [128, CH], F16)), "qw")
            for wt_d, wq_d, wsT in ((w1t_d, w1q_d, w1s), (w2t_d, w2q_d, w2s)):
                nelem = 8 * 4096
                nch = nelem // CH
                for ci in range(nch):
                    k.dma("wl", stg.ap, wt_d[:, CH * ci:CH * (ci + 1)], writes=[stg])
                    k.run("act", lambda e, s=stg, ci=ci: e.activation(
                        out=s.ap, in_=s.ap, func=AF.Abs, accum_out=stat[:, ci:ci + 1]),
                        reads=[stg], writes=[stg, stat])
                k.run("pe", lambda e, nch=nch: e.matmul(pst[0:1, 0:1], onesc.ap, stat[:, 0:nch],
                                                        start=True, stop=True),
                      reads=[onesc, stat], writes=[pst]) if False else None
                k.run("dve", lambda e, nch=nch: e.tensor_reduce(out=stat[:, 4:5], in_=stat[:, 0:nch],
                                                                axis=AXL.X, op=ALU.add),
                      reads=[stat], writes=[stat])
                k.run("pe", lambda e: e.matmul(pst[0:1, 0:1], onesc.ap, stat[:, 4:5], start=True, stop=True),
                      reads=[onesc, stat], writes=[pst])
                k.run("dve", lambda e, n=nelem: e.tensor_scalar(out=junk2[0:1, 0:1], in0=pst[0:1, 0:1],
                                                                scalar1=1.0 / (128.0 * n), scalar2=1e-5,
                                                                op0=ALU.mult, op1=ALU.max),
                      reads=[pst], writes=[junk2])
                k.run("pe", lambda e: e.matmul(pst[0:128, 1:2], onesr.ap, junk2[0:1, 0:1], start=True, stop=True),
                      reads=[onesr, junk2], writes=[pst])
                k.run("dve", lambda e: e.tensor_copy(out=stat[:, 1:2], in_=pst[0:128, 1:2]),
                      reads=[pst], writes=[stat])
                k.run("dve", lambda e, w=wsT: e.tensor_copy(out=w[:, 0:1], in_=stat[:, 1:2]), reads=[stat], writes=[wsT])
                k.run("dve", lambda e, w=wsT: e.reciprocal(out=w[:, 1:2], in_=stat[:, 1:2]), reads=[stat], writes=[wsT])
                k.run("dve", lambda e: e.tensor_scalar_mul(out=stat[:, 2:3], in0=stat[:, 1:2], scalar1=0.5),
                      reads=[stat], writes=[stat])
                k.run("dve", lambda e: e.tensor_scalar_mul(out=stat[:, 3:4], in0=stat[:, 1:2], scalar1=-0.5),
                      reads=[stat], writes=[stat])
                for ci in range(nch):
                    k.dma("wl", stg.ap, wt_d[:, CH * ci:CH * (ci + 1)], writes=[stg])
                    k.run("dve", lambda e, s=stg: e.tensor_tensor(out=gp_.ap, in0=s.ap,
                                                                  in1=stat[:, 2:3].broadcast_to((128, CH)),
                                                                  op=ALU.is_ge),
                          reads=[stg, stat], writes=[gp_])
                    k.run("dve", lambda e, s=stg: e.tensor_tensor(out=gn_.ap, in0=s.ap,
                                                                  in1=stat[:, 3:4].broadcast_to((128, CH)),
                                                                  op=ALU.is_le),
                          reads=[stg, stat], writes=[gn_])
                    k.run("dve", lambda e: e.tensor_tensor(out=qw.ap, in0=gp_.ap, in1=gn_.ap, op=ALU.subtract),
                          reads=[gp_, gn_], writes=[qw])
                    k.dma("sp1", wq_d[:, CH * ci:CH * (ci + 1)], qw.ap, reads=[qw], writes=[w1q_t if wq_d is w1q_d else w2q_t])
            k.barrier()

        with ExitStack() as actx:
            def asb(name, shape, dt):
                return Tile(actx.enter_context(nc.sbuf_tensor(name, list(shape), dt)), name)

            idb = asb("idb_s", [128, 128], BF16)
            lmS = asb("lmS_s", [128, 128], BF16)
            lmI = asb("lmI_s", [128, 128], BF16)
            sel = asb("sel_s", [8, 1024], F32)
            xtile = [asb(f"xt{i}", [128, C], F32) for i in range(2)]
            k.dma("wl", idb.ap, idb_d, writes=[idb])
            k.dma("wl", lmS.ap, lmS_d, writes=[lmS])
            k.dma("wl", lmI.ap, lmI_d, writes=[lmI])
            k.dma("wl", sel.ap, sel_d, writes=[sel])
            wqkv = asb("wqkv_s", [128, 8 * 3088], BF16)
            wz = asb("wz_s", [128, 8 * 1024], BF16)
            wout = asb("wout_s", [128, 8 * 1024], BF16)
            convw = asb("convw_s", [128, 96], F32)
            convd = asb("convd", [128, 96 * 128], BF16)
            xln = asb("xln", [128, C], BF16)
            xlnT = asb("xlnT", [128, 8 * 128], BF16)
            xctxT = asb("xctxT", [128, 8 * 4], BF16)
            convin = [asb(f"cvi{i}", [128, 24 * 131], BF16) for i in range(2)]
            qkvF_r = [asb(f"qkvF{i}", [128, 24 * 128], BF16) for i in range(2)]
            qkvT_one = asb("qkvT", [128, 24 * 128], BF16)
            qkvT_r = [qkvT_one, qkvT_one]
            _shared_stats = dict(
                bF=asb("bFs", [8, 128], F32), aF=asb("aFs", [8, 128], F32),
                ssq=asb("ssqs", [128, 16], F32), lss=asb("lsss", [128, 16], F32),
                lskF=asb("lskFs", [8, 128], F32), lsqF=asb("lsqFs", [8, 128], F32),
                gF=asb("gFs", [8, 128], F32), gcF=asb("gcFs", [8, 128], F32),
                spbF=asb("spbFs", [8, 128], F32), ngcF=asb("ngcFs", [8, 128], F32),
                tmp8=asb("tmp8s", [8, 128], F32), expt=asb("expts", [8, 128], F32),
                stat=asb("stats_", [128, 8], F32),
            )
            sring = []
            for i_ in range(2):
                d_ = dict(_shared_stats)
                d_.update(
                    E1t=asb(f"E1t{i_}", [8, 128], F32), E2t=asb(f"E2t{i_}", [8, 128], F32),
                    E3t=asb(f"E3t{i_}", [8, 128], F32),
                    tmcols=asb(f"tmcols{i_}", [128, 48], F32),
                )
                sring.append(d_)
            vbeta_r = [asb(f"vbeta{i}", [128, 8 * 128], BF16) for i in range(2)]
            kprime_r = [asb(f"kprime{i}", [128, 8 * 128], BF16) for i in range(2)]
            tset = []
            for par in range(4):
                tset.append(dict(
                    M1=asb(f"M1_{par}", [128, 128], BF16),
                    M2=asb(f"M2_{par}", [128, 128], BF16),
                    B0=asb(f"B0_{par}", [128, 128], BF16),
                    C0=asb(f"C0_{par}", [128, 128], BF16),
                    attnT=asb(f"attnT_{par}", [128, 128], BF16),
                    Bk=[asb(f"Bk{i}_{par}", [128, 128], BF16) for i in range(6)],
                    Ck=[asb(f"Ck{i}_{par}", [128, 128], BF16) for i in range(2)],
                    Wch=[asb(f"Wch{i}_{par}", [128, 128], BF16) for i in range(2)],
                    vnref=[None],
                ))
            Sh = [asb(f"S{h}", [128, 128], BF16) for h in range(8)]
            outb_r = [asb(f"outb{i}", [128, 8 * 128], BF16) for i in range(2)]
            zsil_r = [asb(f"zsil{i}", [128, 8 * 128], BF16) for i in range(2)]
            rms = asb("rms", [128, 8], F32)

            k.dma("cw", wqkv.ap, wqkv_d, writes=[wqkv], ename="pool")
            k.dma("cw", wz.ap, wz_d, writes=[wz], ename="pool")
            k.dma("wl", convw.ap, convw_d, writes=[convw])
            k.dma("cw", wout.ap, wout_d, writes=[wout], ename="pool")
            k.run("dve", lambda e: e.tensor_scalar(out=wout.ap, in0=wout.ap, scalar1=nwc[:, 0:1], scalar2=1.0,
                                                   op0=ALU.mult, op1=ALU.mult),
                  reads=[wout, nwc], writes=[wout])
            k.run("dve", lambda e: e.tensor_copy(
                out=convd.h[:].rearrange("p (a b) -> p a b", b=128),
                in_=convw.h[:].rearrange("p (a b) -> p a b", b=1).broadcast_to((128, 96, 128))),
                reads=[convw], writes=[convd])
            k.run("dve", lambda e: e.tensor_tensor(
                out=convd.h[:].rearrange("p (a b) -> p a b", b=128),
                in0=convd.h[:].rearrange("p (a b) -> p a b", b=128),
                in1=idb.h[:].rearrange("p (a b) -> p a b", a=1).broadcast_to((128, 96, 128)),
                op=ALU.mult),
                reads=[convd, idb], writes=[convd])
            for h in range(8):
                k.run("dve", lambda e, h=h: e.memset(Sh[h].ap, 0.0), writes=[Sh[h]])
            for i_ in range(2):
                k.run("dve", lambda e, i_=i_: e.memset(sring[i_]["E3t"].ap, 0.0), writes=[sring[i_]["E3t"]])
                k.run("dve", lambda e, i_=i_: e.memset(sring[i_]["ssq"].ap, 0.0), writes=[sring[i_]["ssq"]])
            k.run("dve", lambda e: e.memset(outb_r[0].ap, 0.0), writes=[outb_r[0]])
            k.run("dve", lambda e: e.memset(outb_r[1].ap, 0.0), writes=[outb_r[1]])
            k.run("dve", lambda e: e.memset(qkvT_one.ap, 0.0), writes=[qkvT_one])
            k.run("dve", lambda e: e.memset(convin[0].ap, 0.0), writes=[convin[0]])
            k.run("dve", lambda e: e.memset(convin[1].ap, 0.0), writes=[convin[1]])
            xctx = xtile[1]

            def prep_block(x_d, b, main):
                xt = xtile[b % 2]
                cvi = convin[b % 2]
                cvprev = convin[(b + 1) % 2]
                qkvF, qkvT = qkvF_r[b % 2], qkvT_r[b % 2]
                vbeta, kprime, zsil = vbeta_r[b % 2], kprime_r[b % 2], zsil_r[b % 2]
                sr = sring[b % 2]
                bF, aF, ssq, lss = sr["bF"], sr["aF"], sr["ssq"], sr["lss"]
                lskF, lsqF, gF, gcF = sr["lskF"], sr["lsqF"], sr["gF"], sr["gcF"]
                spbF, ngcF, tmp8 = sr["spbF"], sr["ngcF"], sr["tmp8"]
                E1t, E2t, E3t, tmcols, expt, stat = sr["E1t"], sr["E2t"], sr["E3t"], sr["tmcols"], sr["expt"], sr["stat"]
                k.dma("xl", xt.ap, x_d[3 + BLK * b: 3 + BLK * (b + 1), :], writes=[xt])
                k.run("dve", lambda e: e.tensor_reduce(out=stat[:, 0:1], in_=xt.ap, axis=AXL.X, op=ALU.add),
                      reads=[xt], writes=[stat])
                k.run("dve", lambda e: e.tensor_scalar_mul(out=stat[:, 1:2], in0=stat[:, 0:1], scalar1=-1.0 / C),
                      reads=[stat], writes=[stat])
                k.run("act", lambda e: e.activation(out=xln.ap, in_=xt.ap, func=AF.Square,
                                                    bias=stat[:, 1:2], scale=cc[:, 2:3], accum_out=stat[:, 2:3]),
                      reads=[xt, stat, cc], writes=[xln, stat])
                k.run("act", lambda e: e.activation(out=stat[:, 3:4], in_=stat[:, 2:3], func=AF.Sqrt,
                                                    bias=cc[:, 0:1], scale=cc[:, 5:6]),
                      reads=[stat, cc], writes=[stat])
                k.run("dve", lambda e: e.reciprocal(out=stat[:, 4:5], in_=stat[:, 3:4]), reads=[stat], writes=[stat])
                k.run("dve", lambda e: e.tensor_tensor(out=stat[:, 5:6], in0=stat[:, 1:2], in1=stat[:, 4:5],
                                                       op=ALU.mult), reads=[stat], writes=[stat])
                k.run("act", lambda e: e.activation(out=xln.ap, in_=xt.ap, func=AF.Identity,
                                                    bias=stat[:, 5:6], scale=stat[:, 4:5]),
                      reads=[xt, stat], writes=[xln])
                k.dma("tq", xlnT.h[:].rearrange("p (a b) -> p a b", b=128),
                      xln.h[:].rearrange("p (a b) -> p a b", b=128),
                      reads=[xln], writes=[xlnT], transpose=True)
                rows = list(range(24)) if main else list(range(8, 24))
                for ri, r in enumerate(rows):
                    p = pj[ri % 2]
                    for d in range(8):
                        k.run("pe", lambda e, r=r, d=d, p=p: e.matmul(
                            p[:, 0:128], wqkv[:, 3088 * d + 128 * r:3088 * d + 128 * (r + 1)],
                            xlnT[:, 128 * d:128 * (d + 1)], start=(d == 0), stop=(d == 7)),
                            reads=[wqkv, xlnT], writes=[p], inc=(d == 7))
                    k.run("dve", lambda e, r=r, p=p: e.tensor_copy(out=cvi[:, 131 * r + 3:131 * r + 131],
                                                                   in_=p[:, 0:128]),
                          reads=[p], writes=[cvi])
                for bi, (dst, c0) in enumerate(((bF, 3072), (aF, 3080))):
                    p = pj[bi % 2]
                    for d in range(8):
                        k.run("pe", lambda e, d=d, c0=c0, p=p: e.matmul(
                            p[0:8, 128:256], wqkv[:, 3088 * d + c0:3088 * d + c0 + 8],
                            xlnT[:, 128 * d:128 * (d + 1)], start=(d == 0), stop=(d == 7)),
                            reads=[wqkv, xlnT], writes=[p], inc=(d == 7))
                    k.run("dve", lambda e, dst=dst, p=p: e.tensor_copy(out=dst.ap, in_=p[0:8, 128:256]),
                          reads=[p], writes=[dst])
                if main:
                    for nzi in range(2):
                        p = pj[nzi % 2]
                        for d in range(8):
                            k.run("pe", lambda e, d=d, nzi=nzi, p=p: e.matmul(
                                p[:, 0:512], xlnT[:, 128 * d:128 * (d + 1)],
                                wz[:, 1024 * d + 512 * nzi:1024 * d + 512 * (nzi + 1)],
                                start=(d == 0), stop=(d == 7)),
                                reads=[wz, xlnT], writes=[p], inc=(d == 7))
                        k.run("act", lambda e, nzi=nzi, p=p: e.activation(
                            out=zsil[:, 512 * nzi:512 * (nzi + 1)], in_=p[:, 0:512], func=AF.Silu),
                            reads=[p], writes=[zsil])
                if b == 0:
                    if main:
                        k.dma("xl", xctx[0:3, :], x_d[0:3, :], writes=[xctx])
                        k.run("dve", lambda e: e.tensor_reduce(out=stat[0:3, 6:7], in_=xctx[0:3, :], axis=AXL.X,
                                                               op=ALU.add), reads=[xctx], writes=[stat])
                        k.run("dve", lambda e: e.tensor_scalar_mul(out=stat[0:3, 7:8], in0=stat[0:3, 6:7],
                                                                   scalar1=-1.0 / C), reads=[stat], writes=[stat])
                        k.run("act", lambda e: e.activation(out=xln[0:3, 0:C], in_=xctx[0:3, :], func=AF.Square,
                                                            bias=stat[0:3, 7:8], scale=cc[0:3, 2:3],
                                                            accum_out=stat[0:3, 6:7]),
                              reads=[xctx, stat, cc], writes=[xln, stat])
                        k.run("act", lambda e: e.activation(out=stat[0:3, 6:7], in_=stat[0:3, 6:7], func=AF.Sqrt,
                                                            bias=cc[0:3, 0:1], scale=cc[0:3, 5:6]),
                              reads=[stat, cc], writes=[stat])
                        k.run("dve", lambda e: e.reciprocal(out=stat[0:3, 6:7], in_=stat[0:3, 6:7]),
                              reads=[stat], writes=[stat])
                        k.run("dve", lambda e: e.tensor_tensor(out=stat[0:3, 7:8], in0=stat[0:3, 7:8],
                                                               in1=stat[0:3, 6:7], op=ALU.mult),
                              reads=[stat], writes=[stat])
                        k.run("act", lambda e: e.activation(out=xln[0:3, 0:C], in_=xctx[0:3, :], func=AF.Identity,
                                                            bias=stat[0:3, 7:8], scale=stat[0:3, 6:7]),
                              reads=[xctx, stat], writes=[xln])
                        for j in range(8):
                            k.run("pe", lambda e, j=j: e.transpose(pstb[0:128, 0:3],
                                                                   xln[0:3, 128 * j:128 * (j + 1)], idb[0:3, 0:3]),
                                  reads=[xln, idb], writes=[pstb])
                            k.run("dve", lambda e, j=j: e.tensor_copy(out=xctxT[:, 4 * j:4 * j + 3],
                                                                      in_=pstb[0:128, 0:3]),
                                  reads=[pstb], writes=[xctxT])
                        for r in range(24):
                            for d in range(8):
                                k.run("pe", lambda e, r=r, d=d: e.matmul(
                                    pj[0][0:128, 4:7], wqkv[:, 3088 * d + 128 * r:3088 * d + 128 * (r + 1)],
                                    xctxT[:, 4 * d:4 * d + 3], start=(d == 0), stop=(d == 7)),
                                    reads=[wqkv, xctxT], writes=[pj[0]], inc=(d == 7))
                            k.run("dve", lambda e, r=r: e.tensor_copy(out=cvi[:, 131 * r:131 * r + 3],
                                                                      in_=pj[0][0:128, 4:7]),
                                  reads=[pj[0]], writes=[cvi])
                    else:
                        k.run("dve", lambda e: e.memset(
                            cvi.h[:].rearrange("p (a b) -> p a b", b=131)[:, :, 0:3], 0.0), writes=[cvi])
                else:
                    k.run("dve", lambda e: e.tensor_copy(
                        out=cvi.h[:].rearrange("p (a b) -> p a b", b=131)[:, :, 0:3],
                        in_=cvprev.h[:].rearrange("p (a b) -> p a b", b=131)[:, :, 128:131]),
                        reads=[cvprev], writes=[cvi])
                for ri, r in enumerate(rows):
                    p = pj[ri % 2]
                    for j in range(4):
                        k.run("pe", lambda e, r=r, j=j, p=p: e.matmul(
                            p[:, 0:128], convd[:, 128 * (4 * r + j):128 * (4 * r + j + 1)],
                            cvi[:, 131 * r + j:131 * r + j + 128], start=(j == 0), stop=(j == 3)),
                            reads=[convd, cvi], writes=[p], inc=(j == 3))
                    k.run("act", lambda e, r=r, p=p: e.activation(out=qkvF[:, 128 * r:128 * (r + 1)],
                                                                  in_=p[:, 0:128], func=AF.Silu),
                          reads=[p], writes=[qkvF])
                r0, nr = rows[0], len(rows)
                k.dma("tq",
                      qkvT.h[:, 128 * r0:128 * (r0 + nr)].rearrange("p (a b) -> p a b", b=128),
                      qkvF.h[:, 128 * r0:128 * (r0 + nr)].rearrange("p (a b) -> p a b", b=128),
                      reads=[qkvF], writes=[qkvT], transpose=True)
                k.run("act", lambda e: e.activation(out=tmp8.ap, in_=aF.ap, func=AF.Exp,
                                                    bias=dtb.ap, scale=c8[:, 0:1]),
                      reads=[aF, dtb, c8], writes=[tmp8])
                k.run("act", lambda e: e.activation(out=gF.ap, in_=tmp8.ap, func=AF.Ln,
                                                    bias=c8[:, 0:1], scale=c8[:, 0:1]),
                      reads=[tmp8, c8], writes=[gF])
                k.run("dve", lambda e: e.tensor_scalar(out=gF.ap, in0=gF.ap, scalar1=nea.ap, scalar2=1.0,
                                                       op0=ALU.mult, op1=ALU.mult), reads=[gF, nea], writes=[gF])
                k.run("act", lambda e: e.activation(out=tmp8.ap, in_=bF.ap, func=AF.Exp,
                                                    bias=c8[:, 3:4], scale=c8[:, 2:3]),
                      reads=[bF, c8], writes=[tmp8])
                k.run("act", lambda e: e.activation(out=spbF.ap, in_=tmp8.ap, func=AF.Ln,
                                                    bias=c8[:, 0:1], scale=c8[:, 0:1]),
                      reads=[tmp8, c8], writes=[spbF])
                k.run("dve", lambda e: e.tensor_tensor_scan(out=gcF.ap, data0=ones8.ap, data1=gF.ap,
                                                            initial=0.0, op0=ALU.mult, op1=ALU.add),
                      reads=[ones8, gF], writes=[gcF])
                k.run("dve", lambda e: e.tensor_scalar_mul(out=ngcF.ap, in0=gcF.ap, scalar1=-1.0),
                      reads=[gcF], writes=[ngcF])
                for h in range(8):
                    if main:
                        k.run("act", lambda e, h=h: e.activation(out=junk[:, 0:128],
                                                                 in_=qkvT[:, 128 * h:128 * (h + 1)],
                                                                 func=AF.Square, accum_out=ssq[:, 8 + h:9 + h]),
                              reads=[qkvT], writes=[junk, ssq])
                    k.run("act", lambda e, h=h: e.activation(out=junk[:, 0:128],
                                                             in_=qkvT[:, 128 * (8 + h):128 * (9 + h)],
                                                             func=AF.Square, accum_out=ssq[:, h:h + 1]),
                          reads=[qkvT], writes=[junk, ssq])
                k.run("act", lambda e: e.activation(out=lss.ap, in_=ssq.ap, func=AF.Ln,
                                                    bias=cc[:, 1:2], scale=cc[:, 2:3]),
                      reads=[ssq, cc], writes=[lss])
                k.run("pe", lambda e: e.transpose(pj[0][0:8, 0:128], lss[:, 0:8], idf.ap),
                      reads=[lss, idf], writes=[pj[0]])
                k.run("pe", lambda e: e.transpose(pj[0][0:8, 128:256], lss[:, 8:16], idf.ap),
                      reads=[lss, idf], writes=[pj[0]])
                k.run("dve", lambda e: e.tensor_copy(out=lskF.ap, in_=pj[0][0:8, 0:128]),
                      reads=[pj[0]], writes=[lskF])
                k.run("dve", lambda e: e.tensor_copy(out=lsqF.ap, in_=pj[0][0:8, 128:256]),
                      reads=[pj[0]], writes=[lsqF])
                k.run("dve", lambda e: e.scalar_tensor_tensor(out=E1t.ap, in0=lskF.ap, scalar=-0.5,
                                                              in1=gcF.ap, op0=ALU.mult, op1=ALU.add),
                      reads=[lskF, gcF], writes=[E1t])
                k.run("dve", lambda e: e.tensor_tensor(out=E1t.ap, in0=E1t.ap, in1=spbF.ap,
                                                       op=ALU.subtract), reads=[E1t, spbF], writes=[E1t])
                k.run("dve", lambda e: e.scalar_tensor_tensor(out=E2t.ap, in0=lskF.ap, scalar=-0.5,
                                                              in1=ngcF.ap, op0=ALU.mult, op1=ALU.add),
                      reads=[lskF, ngcF], writes=[E2t])
                if main:
                    k.run("dve", lambda e: e.scalar_tensor_tensor(out=E3t.ap, in0=lsqF.ap, scalar=-0.5,
                                                                  in1=gcF.ap, op0=ALU.mult, op1=ALU.add),
                          reads=[lsqF, gcF], writes=[E3t])
                plans = [(E1t, "plain", 0), (E3t, "s", 8), (E2t, "gl", 16), (None, "glonly", 24), (spbF, "neg", 32)]
                for srct, mode, col in plans:
                    if mode == "gl":
                        k.run("act", lambda e, s=srct: e.activation(out=expt.ap, in_=s.ap, func=AF.Exp,
                                                                    bias=gcF[:, 127:128], scale=c8[:, 0:1]),
                              reads=[srct, gcF, c8], writes=[expt])
                    elif mode == "glonly":
                        k.run("act", lambda e: e.activation(out=expt.ap, in_=ones8.ap, func=AF.Exp,
                                                            bias=gcF[:, 127:128], scale=c8[:, 3:4]),
                              reads=[ones8, gcF, c8], writes=[expt])
                    elif mode == "neg":
                        k.run("act", lambda e, s=srct: e.activation(out=expt.ap, in_=s.ap, func=AF.Exp,
                                                                    bias=c8[:, 3:4], scale=c8[:, 2:3]),
                              reads=[srct, c8], writes=[expt])
                    elif mode == "s":
                        k.run("act", lambda e, s=srct: e.activation(out=expt.ap, in_=s.ap, func=AF.Exp,
                                                                    bias=c8[:, 1:2], scale=c8[:, 0:1]),
                              reads=[srct, c8], writes=[expt])
                    else:
                        k.run("act", lambda e, s=srct: e.activation(out=expt.ap, in_=s.ap, func=AF.Exp,
                                                                    bias=c8[:, 3:4], scale=c8[:, 0:1]),
                              reads=[srct, c8], writes=[expt])
                    k.run("pe", lambda e, col=col: e.transpose(pj[1][0:128, 256 + ((col // 8) % 2) * 8:256 + ((col // 8) % 2) * 8 + 8], expt.ap, idf[0:8, 0:8]),
                          reads=[expt, idf], writes=[pj[1]])
                    k.run("dve", lambda e, col=col: e.tensor_copy(out=tmcols[:, col:col + 8],
                                                                  in_=pj[1][0:128, 256 + ((col // 8) % 2) * 8:256 + ((col // 8) % 2) * 8 + 8]),
                          reads=[pj[1]], writes=[tmcols])
                k.run("pe", lambda e: e.transpose(pj[1][0:128, 272:280], E2t.ap, idf[0:8, 0:8]),
                      reads=[E2t, idf], writes=[pj[1]])
                k.run("dve", lambda e: e.tensor_copy(out=tmcols[:, 40:48], in_=pj[1][0:128, 272:280]),
                      reads=[pj[1]], writes=[tmcols])
                for h in range(8):
                    k.run("dve", lambda e, h=h: e.tensor_scalar(out=vbeta[:, 128 * h:128 * (h + 1)],
                                                                in0=qkvT[:, 128 * (16 + h):128 * (17 + h)],
                                                                scalar1=tmcols[:, 32 + h:33 + h], scalar2=1.0,
                                                                op0=ALU.mult, op1=ALU.mult),
                          reads=[qkvT, tmcols], writes=[vbeta])
                    k.run("dve", lambda e, h=h: e.tensor_scalar(out=kprime[:, 128 * h:128 * (h + 1)],
                                                                in0=qkvT[:, 128 * (8 + h):128 * (9 + h)],
                                                                scalar1=tmcols[:, 16 + h:17 + h], scalar2=1.0,
                                                                op0=ALU.mult, op1=ALU.mult),
                          reads=[qkvT, tmcols], writes=[kprime])

            def scan_block(x_d, b, main):
                xt = xtile[b % 2]
                qkvF, qkvT = qkvF_r[b % 2], qkvT_r[b % 2]
                vbeta, kprime, zsil = vbeta_r[b % 2], kprime_r[b % 2], zsil_r[b % 2]
                sr = sring[b % 2]
                E1t, E2t, E3t, tmcols = sr["E1t"], sr["E2t"], sr["E3t"], sr["tmcols"]
                outb = outb_r[b % 2]
                def head_steps(h, main):
                    par = h % 4
                    T = tset[par]
                    P = (pc1, pc2, pcv[0], pst)[par]
                    PBsl = pstb[:, 128 * par:128 * (par + 1)]
                    M1_, M2_, B0_, C0_ = T["M1"], T["M2"], T["B0"], T["C0"]
                    attnT_, Bk_, Ck_, Wch_, vnref = T["attnT"], T["Bk"], T["Ck"], T["Wch"], T["vnref"]
                    Ssl = Sh[h]
                    kf = qkvF[:, 128 * (8 + h):128 * (9 + h)]
                    qf = qkvF[:, 128 * h:128 * (h + 1)]
                    A, B, Cs, D = P[:, 0:128], P[:, 128:256], P[:, 256:384], P[:, 384:512]
                    st = []
                    st.append(lambda: k.run("pe", lambda e: e.matmul(A, kf, kf, start=True, stop=True),
                                            reads=[qkvF], writes=[P]))
                    if main:
                        st.append(lambda: k.run("pe", lambda e: e.matmul(B, kf, qf, start=True, stop=True),
                                                reads=[qkvF], writes=[P]))
                    def m1grp():
                        k.run("pe", lambda e: e.matmul(Cs, sel[:, 128 * h:128 * (h + 1)], E1t.ap,
                                                       start=True, stop=False),
                              reads=[sel, E1t], writes=[P], inc=False)
                        k.run("pe", lambda e: e.matmul(Cs, idb.ap, lmS.ap, start=False, stop=True),
                              reads=[idb, lmS], writes=[P])
                    st.append(m1grp)
                    st.append(lambda: k.run("act", lambda e: e.activation(out=M1_.ap, in_=Cs, func=AF.Exp,
                                                                          bias=tmcols[:, 40 + h:41 + h],
                                                                          scale=cc[:, 2:3]),
                                            reads=[P, tmcols, cc], writes=[M1_]))
                    st.append(lambda: k.run("dve", lambda e: e.tensor_tensor(out=B0_.ap, in0=A, in1=M1_.ap,
                                                                             op=ALU.mult),
                                            reads=[P, M1_], writes=[B0_]))
                    if main:
                        def m2grp():
                            k.run("pe", lambda e: e.matmul(Cs, sel[:, 128 * h:128 * (h + 1)], E3t.ap,
                                                           start=True, stop=False),
                                  reads=[sel, E3t], writes=[P], inc=False)
                            k.run("pe", lambda e: e.matmul(Cs, idb.ap, lmI.ap, start=False, stop=True),
                                  reads=[idb, lmI], writes=[P])
                        st.append(m2grp)
                        st.append(lambda: k.run("act", lambda e: e.activation(out=M2_.ap, in_=Cs, func=AF.Exp,
                                                                              bias=tmcols[:, 40 + h:41 + h],
                                                                              scale=cc[:, 2:3]),
                                                reads=[P, tmcols, cc], writes=[M2_]))
                        st.append(lambda: k.run("dve", lambda e: e.tensor_tensor(out=attnT_.ap, in0=B,
                                                                                 in1=M2_.ap, op=ALU.mult),
                                                reads=[P, M2_], writes=[attnT_]))
                    st.append(lambda: k.run("pe", lambda e: e.transpose(PBsl, B0_.ap, idb.ap),
                                            reads=[B0_, idb], writes=[pstb]))
                    st.append(lambda: k.run("dve", lambda e: e.tensor_copy(out=C0_.ap, in_=PBsl),
                                            reads=[pstb], writes=[C0_]))
                    prevB, prevC = [B0_], [C0_]

                    def mkpow(lev, sl, lhs, rhs, dst, eng):
                        def f():
                            a_, b_ = lhs[0], rhs[0]
                            k.run("pe", lambda e, a_=a_, b_=b_: e.matmul(sl, a_.ap, b_.ap, start=True, stop=True),
                                  reads=[a_, b_], writes=[P])
                            if eng == "dve":
                                k.run("dve", lambda e: e.tensor_copy(out=dst.ap, in_=sl), reads=[P], writes=[dst])
                            else:
                                k.run("act", lambda e: e.activation(out=dst.ap, in_=sl, func=AF.Copy),
                                      reads=[P], writes=[dst])
                        return f
                    for lev in range(6):
                        st.append(mkpow(lev, Cs, prevC, prevB, Bk_[lev], "dve" if lev % 2 == 0 else "act"))
                        if lev < 5:
                            st.append(mkpow(lev, B, prevB, prevC, Ck_[lev % 2], "act" if lev % 2 == 0 else "dve"))

                            def upd(lev=lev):
                                prevB[0] = Bk_[lev]
                                prevC[0] = Ck_[lev % 2]
                            st.append(upd)
                    def w0():
                        k.run("pe", lambda e: e.matmul(D, kf, Ssl.ap, start=True, stop=True),
                              reads=[qkvF, Ssl], writes=[P])
                        k.run("dve", lambda e: e.scalar_tensor_tensor(
                            out=Wch_[0].ap, in0=D, scalar=tmcols[:, h:h + 1],
                            in1=vbeta[:, 128 * h:128 * (h + 1)], op0=ALU.mult, op1=ALU.subtract),
                            reads=[P, tmcols, vbeta], writes=[Wch_[0]])
                    st.append(w0)
                    def slv0():
                        k.run("pe", lambda e: e.matmul(D, B0_.ap, Wch_[0].ap, start=True, stop=True),
                              reads=[B0_, Wch_[0]], writes=[P])
                        k.run("dve", lambda e: e.tensor_tensor(out=Wch_[1].ap, in0=Wch_[0].ap, in1=D,
                                                               op=ALU.subtract),
                              reads=[Wch_[0], P], writes=[Wch_[1]])
                    st.append(slv0)
                    cur = [1]
                    for lev in range(6):
                        def slv(lev=lev):
                            wsrc = Wch_[cur[0]]
                            if lev % 2 == 0:
                                k.run("pe", lambda e, w=wsrc: e.matmul(D, Bk_[lev].ap, w.ap,
                                                                       start=True, stop=True),
                                      reads=[Bk_[lev], wsrc], writes=[P])
                                if lev < 5:
                                    dst = Wch_[1 - cur[0]]
                                    k.run("dve", lambda e, w=wsrc, d=dst: e.tensor_tensor(
                                        out=d.ap, in0=w.ap, in1=D, op=ALU.add),
                                        reads=[wsrc, P], writes=[dst])
                                    cur[0] = 1 - cur[0]
                                else:
                                    pass
                            else:
                                k.run("pe", lambda e, w=wsrc: e.matmul(D, Bk_[lev].ap, w.ap,
                                                                       start=True, stop=False),
                                      reads=[Bk_[lev], wsrc], writes=[P], inc=False)
                                k.run("pe", lambda e, w=wsrc: e.matmul(D, idb.ap, w.ap,
                                                                       start=False, stop=True),
                                      reads=[idb, wsrc], writes=[P])
                                if lev < 5:
                                    dst = Wch_[1 - cur[0]]
                                    k.run("act", lambda e, d=dst: e.activation(out=d.ap, in_=D, func=AF.Copy),
                                          reads=[P], writes=[dst])
                                    cur[0] = 1 - cur[0]
                                else:
                                    vt = Wch_[1 - cur[0]]
                                    vnref[0] = vt
                                    k.run("act", lambda e, vt=vt: e.activation(out=vt.ap, in_=D, func=AF.Identity,
                                                                               scale=cc[:, 7:8]),
                                          reads=[P, cc], writes=[vt])
                        st.append(slv)
                    if main:
                        def outs():
                            vn_ = vnref[0]
                            k.run("pe", lambda e, vn_=vn_: e.matmul(Cs, attnT_.ap, vn_.ap, start=True, stop=True),
                                  reads=[attnT_, vn_], writes=[P])
                            k.run("dve", lambda e: e.tensor_copy(out=M2_.ap, in_=Cs), reads=[P], writes=[M2_])
                            k.run("pe", lambda e: e.matmul(D, qf, Ssl.ap, start=True, stop=True),
                                  reads=[qkvF, Ssl], writes=[P])
                            k.run("dve", lambda e: e.scalar_tensor_tensor(
                                out=outb[:, 128 * h:128 * (h + 1)], in0=D,
                                scalar=tmcols[:, 8 + h:9 + h], in1=M2_.ap, op0=ALU.mult, op1=ALU.add),
                                reads=[P, tmcols, M2_], writes=[outb])
                        st.append(outs)
                    def supd():
                        vn_ = vnref[0]
                        k.run("pe", lambda e, vn_=vn_: e.matmul(D, kprime[:, 128 * h:128 * (h + 1)], vn_.ap,
                                                                start=True, stop=True),
                              reads=[kprime, vn_], writes=[P])
                        k.run("dve", lambda e: e.scalar_tensor_tensor(
                            out=Ssl.ap, in0=Ssl.ap, scalar=tmcols[:, 24 + h:25 + h], in1=D,
                            op0=ALU.mult, op1=ALU.add),
                            reads=[Ssl, tmcols, P], writes=[Ssl])
                    st.append(supd)
                    return st

                pthunks = list(scan_block.pending)
                scan_block.pending = []
                psteps = max(1, len(pthunks) // 18) if pthunks else 0
                pi = 0
                for hp in range(2):
                    chains = [head_steps(4 * hp + c, main) for c in range(4)]
                    idxs = [0] * 4
                    while any(idxs[c] < len(chains[c]) for c in range(4)):
                        for c in range(4):
                            if idxs[c] < len(chains[c]):
                                chains[c][idxs[c]]()
                                idxs[c] += 1
                        for _ in range(psteps):
                            if pi < len(pthunks):
                                pthunks[pi]()
                                pi += 1
                while pi < len(pthunks):
                    pthunks[pi]()
                    pi += 1

            def norm_block(b):
                xt = xtile[b % 2]
                zsil = zsil_r[b % 2]
                outb = outb_r[b % 2]
                if True:
                    hgT = xln
                    hgate = outb
                    for h in range(8):
                        k.run("act", lambda e, h=h: e.activation(out=junk[:, 0:128],
                                                                 in_=outb[:, 128 * h:128 * (h + 1)],
                                                                 func=AF.Square, accum_out=rms[:, h:h + 1]),
                              reads=[outb], writes=[junk, rms])
                    k.run("act", lambda e: e.activation(out=rms.ap, in_=rms.ap, func=AF.Sqrt,
                                                        bias=cc[:, 1:2], scale=cc[:, 6:7]),
                          reads=[rms, cc], writes=[rms])
                    k.run("dve", lambda e: e.reciprocal(out=rms.ap, in_=rms.ap), reads=[rms], writes=[rms])
                    for h in range(8):
                        k.run("dve", lambda e, h=h: e.tensor_scalar(out=hgate[:, 128 * h:128 * (h + 1)],
                                                                    in0=outb[:, 128 * h:128 * (h + 1)],
                                                                    scalar1=rms[:, h:h + 1], scalar2=1.0,
                                                                    op0=ALU.mult, op1=ALU.mult),
                              reads=[outb, rms], writes=[hgate])
                    k.run("dve", lambda e: e.tensor_tensor(out=hgate.ap, in0=hgate.ap, in1=zsil.ap,
                                                           op=ALU.mult), reads=[hgate, zsil], writes=[hgate])
                    k.dma("tq", hgT.h[:].rearrange("p (a b) -> p a b", b=128),
                          hgate.h[:].rearrange("p (a b) -> p a b", b=128),
                          reads=[hgate], writes=[hgT], transpose=True)
                    for half in range(2):
                        p = pj[half]
                        for d in range(8):
                            k.run("pe", lambda e, d=d, half=half, p=p: e.matmul(
                                p[:, 0:512], hgT[:, 128 * d:128 * (d + 1)],
                                wout[:, 1024 * d + 512 * half:1024 * d + 512 * (half + 1)],
                                start=(d == 0), stop=(d == 7)),
                                reads=[hgT, wout], writes=[p], inc=(d == 7))
                        k.run("dve", lambda e, half=half, p=p, xt=xt: e.tensor_tensor(
                            out=xt[:, 512 * half:512 * (half + 1)], in0=p[:, 0:512],
                            in1=xt[:, 512 * half:512 * (half + 1)], op=ALU.add),
                            reads=[p, xt], writes=[xt])
                    k.dma("sp1", xmid_d[BLK * b:BLK * (b + 1), :], xt.ap, reads=[xt, xmid_t], writes=[xmid_t])

            scan_block.pending = []
            prep_block(xp_d, 0, main=False)
            for b in range(nb_pref):
                k.begin_defer()
                if b + 1 < nb_pref:
                    prep_block(xp_d, b + 1, main=False)
                else:
                    prep_block(xm_d, 0, main=True)
                scan_block.pending = k.end_defer()
                scan_block(xp_d, b, main=False)
            for b in range(nb_main):
                k.begin_defer()
                if b > 0:
                    norm_block(b - 1)
                if b + 1 < nb_main:
                    prep_block(xm_d, b + 1, main=True)
                scan_block.pending = k.end_defer()
                scan_block(xm_d, b, main=True)
            norm_block(nb_main - 1)
            k.barrier()

        with ExitStack() as fctx:
            def fsb(name, shape, dt):
                return Tile(fctx.enter_context(nc.sbuf_tensor(name, list(shape), dt)), name)

            b1row, b2row = bbrow, bbrow
            w1q = fsb("w1q_s", [128, 8 * 4096], F16)
            w2q = fsb("w2q_s", [128, 32 * 1024], F16)
            xmf = [fsb(f"xmf{i}", [128, C], F32) for i in range(2)]
            xc = fsb("xc", [128, C], F32)
            tq1 = fsb("tq1", [128, C], F32)
            qi1 = fsb("qi1", [128, C], I32)
            qf1 = xc
            q16a_r = [fsb(f"q16a{i}", [128, C], F16) for i in range(2)]
            q16aT_r = [fsb(f"q16aT{i}", [128, C], F16) for i in range(2)]
            g1 = fsb("g1", [128, HF], F32)
            QCH = 512
            tq2 = fsb("tq2", [128, QCH], F32)
            qi2 = fsb("qi2", [128, QCH], I32)
            qf2 = fsb("qf2", [128, QCH], F32)
            q16b = fsb("q16b", [128, QCH], F16)
            q16bT = fsb("q16bT", [128, HF], F16)
            invr_r = [fsb(f"invr{i}", [1, 128], BF16) for i in range(2)]
            statf_r = [fsb(f"statf{i}", [128, 8], F32) for i in range(2)]
            otile_r = [fsb(f"otile{i}", [128, C], F32) for i in range(2)]

            for t_ in (otile_r[0], otile_r[1], g1, xc, tq1, q16aT_r[0], q16aT_r[1], q16bT, q16a_r[0], q16a_r[1], tq2, q16b, invr_r[0], invr_r[1], statf_r[0], statf_r[1], qi1, qi2, xmf[0], xmf[1]):
                k.run("dve", lambda e, t_=t_: e.memset(t_.ap, 0.0), writes=[t_])
            k.dma("wl", w1q.ap, w1q_d, reads=[w1q_t], writes=[w1q])
            k.dma("wl", w2q.ap, w2q_d, reads=[w2q_t], writes=[w2q])

            def ffn_front(b):
                xt = xmf[b % 2]
                stat = statf_r[b % 2]
                invr = invr_r[b % 2]
                q16a, q16aT = q16a_r[b % 2], q16aT_r[b % 2]
                k.dma("xl", xt.ap, xmid_d[BLK * b:BLK * (b + 1), :], reads=[xmid_t], writes=[xt])
                k.run("dve", lambda e, xt=xt: e.tensor_reduce(out=stat[:, 0:1], in_=xt.ap, axis=AXL.X, op=ALU.add),
                      reads=[xt], writes=[stat])
                k.run("dve", lambda e: e.tensor_scalar_mul(out=stat[:, 1:2], in0=stat[:, 0:1], scalar1=-1.0 / C),
                      reads=[stat], writes=[stat])
                k.run("act", lambda e, xt=xt: e.activation(out=xc.ap, in_=xt.ap, func=AF.Identity,
                                                           bias=stat[:, 1:2], scale=cc[:, 2:3]),
                      reads=[xt, stat, cc], writes=[xc])
                k.run("act", lambda e: e.activation(out=tq1.ap, in_=xc.ap, func=AF.Square,
                                                    accum_out=stat[:, 2:3]), reads=[xc], writes=[tq1, stat])
                k.run("act", lambda e: e.activation(out=stat[:, 3:4], in_=stat[:, 2:3], func=AF.Sqrt,
                                                    bias=cc[:, 0:1], scale=cc[:, 5:6]),
                      reads=[stat, cc], writes=[stat])
                k.run("dve", lambda e: e.reciprocal(out=stat[:, 4:5], in_=stat[:, 3:4]), reads=[stat], writes=[stat])
                k.run("dve", lambda e: e.tensor_reduce(out=stat[:, 5:6], in_=xc.ap, axis=AXL.X, op=ALU.max,
                                                       apply_absolute_value=True), reads=[xc], writes=[stat])
                k.run("dve", lambda e: e.tensor_scalar(out=stat[:, 5:6], in0=stat[:, 5:6], scalar1=1e-30,
                                                       scalar2=1.0, op0=ALU.max, op1=ALU.mult),
                      reads=[stat], writes=[stat])
                k.run("dve", lambda e: e.reciprocal(out=stat[:, 6:7], in_=stat[:, 5:6]), reads=[stat], writes=[stat])
                k.run("dve", lambda e: e.tensor_scalar_mul(out=stat[:, 6:7], in0=stat[:, 6:7], scalar1=127.0),
                      reads=[stat], writes=[stat])
                k.run("act", lambda e: e.activation(out=tq1.ap, in_=xc.ap, func=AF.Identity,
                                                    bias=cc[:, 3:4], scale=stat[:, 6:7]),
                      reads=[xc, stat, cc], writes=[tq1])
                k.run("dve", lambda e: e.tensor_copy(out=qi1.ap, in_=tq1.ap), reads=[tq1], writes=[qi1])
                k.run("dve", lambda e: e.tensor_copy(out=qf1.ap, in_=qi1.ap), reads=[qi1], writes=[qf1])
                k.run("dve", lambda e: e.tensor_tensor(out=q16a.ap, in0=qf1.ap, in1=tq1.ap, op=ALU.is_gt),
                      reads=[qf1, tq1], writes=[q16a])
                k.run("dve", lambda e: e.scalar_tensor_tensor(out=q16a.ap, in0=qf1.ap, scalar=-256.0,
                                                              in1=q16a.ap, op0=ALU.add, op1=ALU.subtract),
                      reads=[qf1, q16a], writes=[q16a])
                k.dma("tq", q16aT.h[:].rearrange("p (a b) -> p a b", b=128),
                      q16a.h[:].rearrange("p (a b) -> p a b", b=128),
                      reads=[q16a], writes=[q16aT], transpose=True)
                k.run("dve", lambda e: e.tensor_tensor(out=stat[:, 7:8], in0=stat[:, 5:6], in1=stat[:, 4:5],
                                                       op=ALU.mult), reads=[stat], writes=[stat])
                k.run("dve", lambda e: e.tensor_scalar(out=stat[:, 7:8], in0=stat[:, 7:8], scalar1=1e-5,
                                                       scalar2=1.0 / 127.0, op0=ALU.max, op1=ALU.mult),
                      reads=[stat], writes=[stat])
                k.run("dve", lambda e: e.tensor_tensor(out=stat[:, 7:8], in0=stat[:, 7:8], in1=w1s[:, 0:1],
                                                       op=ALU.mult), reads=[stat, w1s], writes=[stat])
                k.run("dve", lambda e: e.reciprocal(out=stat[:, 0:1], in_=stat[:, 7:8]), reads=[stat], writes=[stat])
                k.run("pe", lambda e: e.transpose(pcv[0][0:1, 0:128], stat[:, 0:1], idf.ap),
                      reads=[stat, idf], writes=[pcv[0]])
                k.run("dve", lambda e: e.tensor_copy(out=invr.ap, in_=pcv[0][0:1, 0:128]),
                      reads=[pcv[0]], writes=[invr])
            def ffn_back(b):
                xt = xmf[b % 2]
                stat = statf_r[b % 2]
                invr = invr_r[b % 2]
                otile = otile_r[b % 2]
                q16aT = q16aT_r[b % 2]
                pthunks = list(ffn_back.pending)
                ffn_back.pending = []
                pstate = [0]
                def drain(n_):
                    e_ = min(pstate[0] + n_, len(pthunks))
                    while pstate[0] < e_:
                        pthunks[pstate[0]]()
                        pstate[0] += 1
                nper = -(-len(pthunks) // 10) if pthunks else 0
                for n in range(8):
                    p = pj[n % 2]
                    for d in range(8):
                        k.run("pe", lambda e, d=d, n=n, p=p: e.matmul(
                            p[:, 0:512], q16aT[:, 128 * d:128 * (d + 1)],
                            w1q[:, 4096 * d + 512 * n:4096 * d + 512 * (n + 1)],
                            start=(d == 0), stop=False),
                            reads=[q16aT, w1q], writes=[p], inc=False)
                    k.run("pe", lambda e, n=n, p=p: e.matmul(
                        p[:, 0:512], invr.ap, b1row[0:1, 512 * n:512 * (n + 1)], start=False, stop=True),
                        reads=[invr, b1row], writes=[p])
                    k.run("act", lambda e, n=n, p=p: e.activation(out=g1[:, 512 * n:512 * (n + 1)],
                                                                  in_=p[:, 0:512], func=AF.Gelu,
                                                                  scale=stat[:, 7:8]),
                          reads=[p, stat], writes=[g1])
                    drain(nper)
                k.run("dve", lambda e: e.tensor_reduce(out=stat[:, 5:6], in_=g1.ap, axis=AXL.X, op=ALU.max,
                                                       apply_absolute_value=True), reads=[g1], writes=[stat])
                k.run("dve", lambda e: e.tensor_scalar(out=stat[:, 5:6], in0=stat[:, 5:6], scalar1=1e-30,
                                                       scalar2=1.0, op0=ALU.max, op1=ALU.mult),
                      reads=[stat], writes=[stat])
                k.run("dve", lambda e: e.reciprocal(out=stat[:, 6:7], in_=stat[:, 5:6]), reads=[stat], writes=[stat])
                k.run("dve", lambda e: e.tensor_scalar_mul(out=stat[:, 6:7], in0=stat[:, 6:7], scalar1=127.0),
                      reads=[stat], writes=[stat])
                for qc in range(HF // QCH):
                    k.run("act", lambda e, qc=qc: e.activation(out=tq2.ap, in_=g1[:, QCH * qc:QCH * (qc + 1)],
                                                               func=AF.Identity, bias=cc[:, 3:4], scale=stat[:, 6:7]),
                          reads=[g1, stat, cc], writes=[tq2])
                    k.run("dve", lambda e: e.tensor_copy(out=qi2.ap, in_=tq2.ap), reads=[tq2], writes=[qi2])
                    k.run("dve", lambda e: e.tensor_copy(out=qf2[:, 0:QCH], in_=qi2.ap), reads=[qi2], writes=[qf2])
                    k.run("dve", lambda e: e.tensor_tensor(out=q16b.ap, in0=qf2[:, 0:QCH], in1=tq2.ap, op=ALU.is_gt),
                          reads=[qf2, tq2], writes=[q16b])
                    k.run("dve", lambda e: e.scalar_tensor_tensor(out=q16b.ap, in0=qf2[:, 0:QCH], scalar=-256.0,
                                                                  in1=q16b.ap, op0=ALU.add, op1=ALU.subtract),
                          reads=[qf2, q16b], writes=[q16b])
                    k.dma("tq",
                          q16bT.h[:, QCH * qc:QCH * (qc + 1)].rearrange("p (a b) -> p a b", b=128),
                          q16b.h[:].rearrange("p (a b) -> p a b", b=128),
                          reads=[q16b], writes=[q16bT], transpose=True)
                    drain(nper)
                drain(len(pthunks))
                k.run("dve", lambda e: e.tensor_scalar(out=stat[:, 7:8], in0=stat[:, 5:6], scalar1=1e-5,
                                                       scalar2=1.0 / 127.0, op0=ALU.max, op1=ALU.mult),
                      reads=[stat], writes=[stat])
                k.run("dve", lambda e: e.tensor_tensor(out=stat[:, 7:8], in0=stat[:, 7:8], in1=w2s[:, 0:1],
                                                       op=ALU.mult), reads=[stat, w2s], writes=[stat])
                k.run("dve", lambda e: e.reciprocal(out=stat[:, 0:1], in_=stat[:, 7:8]), reads=[stat], writes=[stat])
                k.run("pe", lambda e: e.transpose(pst[0:1, 0:128], stat[:, 0:1], idf.ap),
                      reads=[stat, idf], writes=[pst])
                k.run("dve", lambda e: e.tensor_copy(out=invr.ap, in_=pst[0:1, 0:128]),
                      reads=[pst], writes=[invr])
                for n in range(2):
                    p = pj[n % 2]
                    for d in range(32):
                        k.run("pe", lambda e, d=d, n=n, p=p: e.matmul(
                            p[:, 0:512], q16bT[:, 128 * d:128 * (d + 1)],
                            w2q[:, 1024 * d + 512 * n:1024 * d + 512 * (n + 1)],
                            start=(d == 0), stop=False),
                            reads=[q16bT, w2q], writes=[p], inc=False)
                    k.run("pe", lambda e, n=n, p=p: e.matmul(
                        p[:, 0:512], invr.ap, b2row[0:1, HF + 512 * n:HF + 512 * (n + 1)], start=False, stop=True),
                        reads=[invr, b2row], writes=[p])
                    k.run("dve", lambda e, n=n, p=p, xt=xt: e.scalar_tensor_tensor(
                        out=otile[:, 512 * n:512 * (n + 1)], in0=p[:, 0:512], scalar=stat[:, 7:8],
                        in1=xt[:, 512 * n:512 * (n + 1)], op0=ALU.mult, op1=ALU.add),
                        reads=[p, stat, xt], writes=[otile])
                k.dma("st", out_d[BLK * b:BLK * (b + 1), :], otile.ap, reads=[otile])

            ffn_back.pending = []
            ffn_front(0)
            for b in range(nb_main):
                k.begin_defer()
                if b + 1 < nb_main:
                    ffn_front(b + 1)
                ffn_back.pending = k.end_defer()
                ffn_back(b)
            k.barrier()
        k.finalize(block)
    return nc


_PROG_CACHE = {}


def kernel(**inputs):
    nb_main, nb_pref = 32, 32
    per_core = host_prep(inputs, nb_main, nb_pref)
    key = (nb_main, nb_pref)
    if key not in _PROG_CACHE:
        _PROG_CACHE[key] = build_program(nb_main, nb_pref)
    nc = _PROG_CACHE[key]
    res = run_bass_kernel_spmd(nc, per_core, core_ids=list(range(8)))
    out = np.empty((4, 8192, 1024), np.float32)
    for c in range(8):
        b, half = c // 2, c % 2
        out[b, half * 4096:(half + 1) * 4096] = res.results[c]["out"]
    return out
